# version 03
# baseline (speedup 1.0000x reference)
import math
import contextlib
import numpy as np
import ml_dtypes
import concourse.bass as bass
import concourse.mybir as mybir
from concourse.bass_utils import run_bass_kernel_spmd

F32 = mybir.dt.float32
BF16 = mybir.dt.bfloat16
AF = mybir.ActivationFunctionType
ALU = mybir.AluOpType
AX = mybir.AxisListType

T = 2048
D = 1024
NCH = 8
NT = 4
TT = 512
NH = 16
HD = 64
EPS = 1e-5
MASKVAL = -98304.0


def _host_consts():
    c = {}
    c["IDN"] = np.eye(128, dtype=ml_dtypes.bfloat16)
    c["IDNF"] = np.eye(128, dtype=np.float32)
    s32 = np.zeros((128, 128), dtype=ml_dtypes.bfloat16)
    for b in range(4):
        for i in range(16):
            s32[b * 32 + i, b * 32 + 16 + i] = 1
            s32[b * 32 + 16 + i, b * 32 + i] = 1
    c["SWAP32"] = s32
    s64 = np.zeros((128, 128), dtype=ml_dtypes.bfloat16)
    for b in range(2):
        for i in range(32):
            s64[b * 64 + i, b * 64 + 32 + i] = 1
            s64[b * 64 + 32 + i, b * 64 + i] = 1
    c["SWAP64"] = s64
    m = np.zeros((4, 128, TT), dtype=ml_dtypes.bfloat16)
    kk = np.arange(128)[:, None]
    qq = np.arange(TT)[None, :]
    for i, o in enumerate((0, 128, 256, 384)):
        m[i] = np.where(kk + o > qq, MASKVAL, 0.0).astype(ml_dtypes.bfloat16)
    c["MASKS"] = m
    m01 = np.zeros((4, 128, 2 * TT), dtype=ml_dtypes.bfloat16)
    for i, o in enumerate((0, 128, 256, 384)):
        keep = (kk + o <= qq).astype(ml_dtypes.bfloat16)
        m01[i, :, 0:TT] = keep
        m01[i, :, TT:2 * TT] = keep
    c["MASKS01"] = m01
    bm = np.zeros((4, 128, 16), dtype=ml_dtypes.bfloat16)
    for cg in range(4):
        for p in range(128):
            bm[cg, p, 4 * cg + p // 32] = 1
    c["BONES_M"] = bm
    pab = np.zeros((4, 128, 128), dtype=ml_dtypes.bfloat16)
    for var in range(2):
        for i in range(32):
            pab[var, 64 * var + i, i] = 1
            pab[var, 64 * var + 32 + i, 64 + i] = 1
            pab[2 + var, 64 * var + i, 32 + i] = 1
            pab[2 + var, 64 * var + 32 + i, 96 + i] = 1
    c["PAB"] = pab
    bd = np.zeros((8, 128, 16), dtype=ml_dtypes.bfloat16)
    for ch in range(8):
        for p in range(128):
            bd[ch, p, 2 * ch + p // 64] = 1
    c["BONES_D"] = bd
    invf = (10000.0 ** (-np.arange(0, 32, 2, dtype=np.float64) / 32.0))
    tpos = np.arange(T, dtype=np.float64)
    th = tpos[None, :] * invf[:, None]
    cos16 = np.cos(th).astype(np.float32)
    sin16 = np.sin(th).astype(np.float32)
    cs1 = np.zeros((128, T), dtype=np.float32)
    cs2 = np.zeros((128, T), dtype=np.float32)
    for r in range(128):
        j = r % 16
        cs1[r] = cos16[j]
        cs2[r] = -sin16[j] if (r % 32) < 16 else sin16[j]
    c["CS1"] = cs1.astype(ml_dtypes.bfloat16)
    c["CS2"] = cs2.astype(ml_dtypes.bfloat16)
    RZ = (14.134725142, 21.022039639, 25.01085758, 30.424876126,
          32.935061588, 37.586178159, 40.918719012, 43.327073281,
          48.005150881, 49.773832478, 52.970321478, 56.446247697,
          59.347044003, 60.831778525, 65.112544048, 67.079810529,
          69.546401711, 72.067157674, 75.704690699, 77.144840069,
          79.33737502, 82.910380854, 84.735492981, 87.425274613,
          88.809111208, 92.491899271, 94.651344041, 95.870634228,
          98.831194218, 101.317851006, 103.72553804, 105.446623052)
    gamma = np.asarray(RZ, dtype=np.float32)
    invfd = (1.0 / (gamma / gamma[0])).astype(np.float64)
    d0 = np.zeros((128, T), dtype=np.float32)
    for r in range(128):
        d0[r] = (tpos * invfd[r % 32] / (2 * math.pi)).astype(np.float32)
    c["DCONST0"] = d0
    sgn = np.ones((128, 1), dtype=np.float32)
    for r in range(128):
        if (r % 64) < 32:
            sgn[r] = -1.0
    c["SGN"] = sgn
    s7 = np.zeros((7, 7 * 128), dtype=ml_dtypes.bfloat16)
    for e in range(7):
        s7[e, e * 128:(e + 1) * 128] = 1
    c["SEL7"] = s7
    sel = np.zeros((8, 16, 128), dtype=np.float32)
    for ch in range(8):
        for p in range(128):
            sel[ch, 2 * ch + p // 64, p] = 1
    c["SELS"] = sel
    return c


def _prep_weights(inp):
    f32 = np.float32
    w = {}
    A = lambda x: np.ascontiguousarray(np.asarray(x, dtype=f32))
    w["kv_dT"] = A(np.asarray(inp["L0_ffn_kv_d"], f32).T)
    w["q_dT"] = A(np.asarray(inp["L0_ffn_q_d"], f32).T)
    ku = np.asarray(inp["L0_ffn_k_u"], dtype=f32).reshape(NH, HD, 256)
    qu = np.asarray(inp["L0_ffn_q_u"], dtype=f32).reshape(NH, HD, 256)
    kun = ku[:, 32:, :].reshape(NH * 32, 256).T
    qun = qu[:, 32:, :].reshape(NH * 32, 256).T
    kp = np.zeros((256, 1024), dtype=f32)
    qp = np.zeros((256, 1024), dtype=f32)
    for ch in range(8):
        kp[:, ch * 128 + 32:ch * 128 + 64] = kun[:, (2 * ch) * 32:(2 * ch + 1) * 32]
        kp[:, ch * 128 + 96:ch * 128 + 128] = kun[:, (2 * ch + 1) * 32:(2 * ch + 2) * 32]
        qp[:, ch * 128 + 32:ch * 128 + 64] = qun[:, (2 * ch) * 32:(2 * ch + 1) * 32]
        qp[:, ch * 128 + 96:ch * 128 + 128] = qun[:, (2 * ch + 1) * 32:(2 * ch + 2) * 32]
    w["ku_nP"] = A(kp)
    w["qu_nP"] = A(qp)
    w["v_uT"] = A(np.asarray(inp["L0_ffn_v_u"], f32).T)
    w["rope_qT"] = A(np.asarray(inp["L0_ffn_rope_q_w"], f32).T)
    w["rope_kT"] = A(np.asarray(inp["L0_ffn_rope_k_w"], f32).T)
    w["o_wT"] = A(np.asarray(inp["L0_ffn_o_w"], f32).T)
    w["ln1w"] = A(inp["L0_ffn_ln1_w"]).reshape(1, D)
    w["ln2w"] = A(inp["L0_ffn_ln2_w"]).reshape(1, D)
    wqkv = np.asarray(inp["L0_attn_w_qkv"], dtype=f32).reshape(D, 3, NH, HD)
    perm = np.concatenate([np.arange(0, 64, 2), np.arange(1, 64, 2)])
    q = wqkv[:, 0][:, :, perm]
    k = wqkv[:, 1][:, :, perm]
    v = wqkv[:, 2]
    w["w_qkv_p"] = A(np.concatenate(
        [q.reshape(D, D), k.reshape(D, D), v.reshape(D, D)], axis=1))
    w["w_o"] = A(inp["L0_attn_w_o"])
    w["log_scale_c"] = A(inp["L0_attn_log_scale"]).reshape(NH, 1)
    w["router_wT"] = A(np.asarray(inp["L0_ffn_router_w"], f32).T)
    w["rbias"] = A(inp["L0_ffn_routing_bias"]).reshape(7, 1)
    g = np.asarray(inp["L0_ffn_shared_gate"], dtype=f32)
    u = np.asarray(inp["L0_ffn_shared_up"], dtype=f32)
    d = np.asarray(inp["L0_ffn_shared_down"], dtype=f32)
    rg = np.asarray(inp["L0_ffn_routed_gate"], dtype=f32)
    ru = np.asarray(inp["L0_ffn_routed_up"], dtype=f32)
    rd = np.asarray(inp["L0_ffn_routed_down"], dtype=f32)
    w["gT"] = A(np.concatenate([g.transpose(0, 2, 1),
                                rg.transpose(0, 2, 1)], axis=0))
    w["uT"] = A(np.concatenate([u.transpose(0, 2, 1),
                                ru.transpose(0, 2, 1)], axis=0))
    w["dT"] = A(np.concatenate([d.transpose(0, 2, 1),
                                rd.transpose(0, 2, 1)], axis=0))
    return w


class Sem:
    def __init__(self, h):
        self.h = h
        self.v = 0

    def inc(self, n=1):
        self.v += n
        return self.v


_UID = [0]


def U(name):
    _UID[0] += 1
    return f"{name}_{_UID[0]}"


class Prog:

    def __init__(self, nc, want_taps):
        self.nc = nc
        self.want_taps = set(want_taps)
        self.tap_shapes = {}
        self._tapn = 0
        self.sems = {}
        self.bufs = {}
        self.dram = {}
        self.stop_after = None
        self.stop_hit = False
        self.XBF = None

    def tap(self, name, aps):
        if name == self.stop_after:
            self.stop_hit = True
        if name not in self.want_taps:
            return
        nc = self.nc
        shp = [len(aps)] + list(aps[0].shape)
        t = nc.dram_tensor(f"tap_{name}", shp, F32, kind="ExternalOutput").ap()
        self.tap_shapes[name] = shp
        with nc.semaphore(f"tapsem{self._tapn}") as sem:
            self._tapn += 1
            with nc.Block(no_gpsimd_drain=True) as block:
                @block.gpsimd
                def _(gpsimd):
                    for i, ap in enumerate(aps):
                        gpsimd.dma_start(out=t[i], in_=ap).then_inc(sem, 16)
                    gpsimd.wait_ge(sem, 16 * len(aps))
        emit_barrier(nc)


def emit_barrier(nc):
    with nc.Block(no_gpsimd_drain=True) as block:
        @block.sync
        def _(sync):
            sync.nop()


def emit_ln(pr, src_kind, src, dst, w_row, pre_norm_hook=None):
    nc = pr.nc
    b = pr.bufs
    ld, ev, mm, ac = (pr.sems[s] for s in ("ld", "ev", "mm", "ac"))
    onesc = b["ONEC"]
    sqt = b["SQT"]
    rowa = b["ROWA"]
    rowb = b["ROWB"]

    r_r = b["RR"]
    r_n = b["RN"]
    tmp1 = b["TMP1"]
    wrow = w_row

    with contextlib.ExitStack() as ps_scope:
        ps_sum = ps_scope.enter_context(nc.psum_tensor(U("ln_sum"), [1, T], F32))[:]
        ps_sq = ps_scope.enter_context(nc.psum_tensor(U("ln_sq"), [1, T], F32))[:]

        if src_kind == "hbm":
            ld0 = ld.v
            with nc.Block(no_gpsimd_drain=True) as block:
                @block.gpsimd
                def _(gpsimd):
                    for c in range(NCH):
                        gpsimd.dma_start(
                            out=dst[c],
                            in_=src[c * 128:(c + 1) * 128, :]).then_inc(ld.h, 16)
                    ld.inc(16 * NCH)
            xbf = dst
            chunk_ready = lambda c: ld0 + 16 * (c + 1)
        elif src_kind == "acc":
            ev0 = ev.v
            with nc.Block(no_gpsimd_drain=True) as block:
                @block.vector
                def _(vector):
                    for c in range(NCH):
                        vector.tensor_copy(dst[c], src[c]).then_inc(ev.h, 1)
                    ev.inc(NCH)
            xbf = dst
            chunk_ready = lambda c: None
        else:
            xbf = src
            chunk_ready = lambda c: None

        mm0, ac0, ev0b = mm.v, ac.v, ev.v
        with nc.Block(no_gpsimd_drain=True) as block:
            @block.scalar
            def _(scalar):
                for c in range(NCH):
                    if src_kind == "hbm":
                        scalar.wait_ge(ld.h, chunk_ready(c))
                    elif src_kind == "acc":
                        scalar.wait_ge(ev.h, ev0b - NCH + c + 1)
                    if c >= 1:
                        scalar.wait_ge(mm.h, mm0 + c)
                    scalar.activation(sqt, xbf[c], AF.Square).then_inc(ac.h, 1)

            @block.tensor
            def _(tensor):
                for c in range(NCH):
                    if src_kind == "hbm":
                        tensor.wait_ge(ld.h, chunk_ready(c))
                    elif src_kind == "acc":
                        tensor.wait_ge(ev.h, ev0b - NCH + c + 1)
                    for t in range(NT):
                        tensor.matmul(ps_sum[:, t * TT:(t + 1) * TT], onesc,
                                      xbf[c][:, t * TT:(t + 1) * TT],
                                      start=(c == 0), stop=(c == NCH - 1))
                    tensor.wait_ge(ac.h, ac0 + c + 1)
                    for t in range(NT):
                        mmi = tensor.matmul(
                            ps_sq[:, t * TT:(t + 1) * TT], onesc,
                            sqt[:, t * TT:(t + 1) * TT],
                            start=(c == 0), stop=(c == NCH - 1))
                    mmi.then_inc(mm.h, 1)
        mm.inc(NCH)
        ac.inc(NCH)

        with nc.Block(no_gpsimd_drain=True) as block:
            @block.scalar
            def _(scalar):
                scalar.activation(rowa, ps_sum, AF.Copy, scale=1.0 / D)
                scalar.activation(rowb, ps_sq, AF.Copy, scale=1.0 / D)\
                    .then_inc(ac.h, 1)

            @block.vector
            def _(vector):
                vector.wait_ge(ac.h, ac.inc())
                vector.tensor_tensor(ps_sum, rowa, rowa, ALU.mult)
                vector.tensor_tensor(rowb, rowb, ps_sum, ALU.subtract)\
                    .then_inc(ev.h, 1)

            @block.scalar
            def _(scalar):
                scalar.wait_ge(ev.h, ev.inc())
                scalar.activation(rowb, rowb, AF.Ln,
                                  bias=pr.bufs["CONSTS"][0:1, 0:1])
                scalar.activation(r_r, rowb, AF.Exp, scale=-0.5)\
                    .then_inc(ac.h, 1)

            @block.vector
            def _(vector):
                vector.wait_ge(ac.h, ac.inc())
                vector.tensor_tensor(r_n, rowa, r_r, ALU.mult)
                vector.tensor_scalar(r_n, r_n, -1.0, None, ALU.mult)
    emit_barrier(nc)

    if pre_norm_hook is not None:
        pre_norm_hook()

    with contextlib.ExitStack() as ps_scope:
        ps_cd = [ps_scope.enter_context(
            nc.psum_tensor(U(f"ln_cd{i}"), [128, TT], F32))[:] for i in range(4)]
        mm0, ev0 = mm.v, ev.v
        with nc.Block(no_gpsimd_drain=True) as block:
            @block.tensor
            def _(tensor):
                for c in range(NCH):
                    for t in range(NT):
                        i = c * NT + t
                        if i >= 2:
                            tensor.wait_ge(ev.h, ev0 + i - 1)
                        wr = (wrow[:, c * 128:(c + 1) * 128]
                              if wrow is not None else b["ONE1"][0:1])
                        tensor.matmul(ps_cd[(i % 2) * 2], wr,
                                      r_r[:, t * TT:(t + 1) * TT],
                                      start=True, stop=True)
                        tensor.matmul(ps_cd[(i % 2) * 2 + 1], wr,
                                      r_n[:, t * TT:(t + 1) * TT],
                                      start=True, stop=True)\
                            .then_inc(mm.h, 1)

            @block.vector
            def _(vector):
                for c in range(NCH):
                    for t in range(NT):
                        i = c * NT + t
                        vector.wait_ge(mm.h, mm0 + i + 1)
                        sl = slice(t * TT, (t + 1) * TT)
                        vector.tensor_tensor(tmp1, xbf[c][:, sl],
                                             ps_cd[(i % 2) * 2], ALU.mult)
                        vector.tensor_tensor(dst[c][:, sl], tmp1,
                                             ps_cd[(i % 2) * 2 + 1], ALU.add)\
                            .then_inc(ev.h, 1)
        mm.inc(NCH * NT)
        ev.inc(NCH * NT)
    emit_barrier(nc)


def emit_loads(pr, loads, sem="wld"):
    s = pr.sems[sem]
    nc = pr.nc
    targets = []
    with nc.Block(no_gpsimd_drain=True) as block:
        @block.gpsimd
        def _(gpsimd):
            for dst, src in loads:
                gpsimd.dma_start(out=dst, in_=src).then_inc(s.h, 16)
                targets.append(s.inc(16))
    return targets


def emit_proj(pr, jobs, waits=(), evac=None, psn=2, sync_section=None):
    nc = pr.nc
    mm, ev = pr.sems["mm"], pr.sems["ev"]
    mm0, ev0 = mm.v, ev.v
    n = len(jobs)

    def ps_view(pss_i, out_ap):
        v = pss_i
        if out_ap.shape[1] < TT:
            v = v[:, 0:out_ap.shape[1]]
        if out_ap.shape[0] < 128:
            v = v[0:out_ap.shape[0]]
        return v

    with contextlib.ExitStack() as ps_scope:
        pss = [ps_scope.enter_context(
            nc.psum_tensor(U(f"pj{i}"), [128, TT], F32))[:] for i in range(psn)]
        with nc.Block(no_gpsimd_drain=True) as block:
            if sync_section is not None:
                @block.sync
                def _(sync):
                    sync_section(sync)

            @block.tensor
            def _(tensor):
                for i, (out_ap, pieces) in enumerate(jobs):
                    wl = waits(i) if callable(waits) else (waits if i == 0 else ())
                    for sh, tgt in wl:
                        tensor.wait_ge(sh, tgt)
                    if i >= psn:
                        tensor.wait_ge(ev.h, ev0 + i - psn + 1)
                    np_ = len(pieces)
                    for j, piece in enumerate(pieces):
                        tp = piece[2] if len(piece) > 2 else None
                        pv = pss[i % psn] if evac else ps_view(pss[i % psn], out_ap)
                        mmi = tensor.matmul(pv, piece[0], piece[1],
                                            start=(j == 0), stop=(j == np_ - 1),
                                            tile_position=tp)
                    mmi.then_inc(mm.h, 1)

            @block.vector
            def _(vector):
                for i, (out_ap, pieces) in enumerate(jobs):
                    vector.wait_ge(mm.h, mm0 + i + 1)
                    if evac is None:
                        ins = vector.tensor_copy(out_ap,
                                                 ps_view(pss[i % psn], out_ap))
                    else:
                        ins = evac(vector, i, out_ap, pss[i % psn])
                    ins.then_inc(ev.h, 1)
        mm.inc(n)
        ev.inc(n)
    emit_barrier(nc)

def emit_rotate(pr, chunks, cs1, cs2, swap, tag):
    nc = pr.nc
    b = pr.bufs
    mm, ev = pr.sems["mm"], pr.sems["ev"]
    with contextlib.ExitStack() as ps_scope:
        t2 = ps_scope.enter_context(
            nc.sbuf_tensor(U("T2TMP"), [128, 2 * TT], BF16))[:]
        tb = ps_scope.enter_context(
            nc.sbuf_tensor(U("TMPB"), [128, 2 * TT], BF16))[:]
        pss = [ps_scope.enter_context(
            nc.psum_tensor(U(f"rot{tag}{i}"), [128, TT], F32))[:] for i in range(2)]
        for ci, ch in enumerate(chunks):
            mm0, ev0 = mm.v, ev.v
            with nc.Block(no_gpsimd_drain=True) as block:
                @block.tensor
                def _(tensor):
                    for t in range(NT):
                        if t >= 2:
                            tensor.wait_ge(ev.h, ev0 + t - 1)
                        tensor.matmul(pss[t % 2], swap,
                                      ch[:, t * TT:(t + 1) * TT],
                                      start=True, stop=True).then_inc(mm.h, 1)

                @block.vector
                def _(vector):
                    for t in range(NT):
                        sl = slice(t * TT, (t + 1) * TT)
                        t2s = slice((t % 2) * TT, (t % 2 + 1) * TT)
                        vector.wait_ge(mm.h, mm0 + t + 1)
                        vector.tensor_copy(t2[:, t2s], pss[t % 2])\
                            .then_inc(ev.h, 1)
                        vector.tensor_tensor(tb[:, 0:TT], ch[:, sl],
                                             cs1[:, sl], ALU.mult)
                        vector.tensor_tensor(tb[:, TT:2 * TT], t2[:, t2s],
                                             cs2[:, sl], ALU.mult)
                        vector.tensor_tensor(ch[:, sl], tb[:, 0:TT],
                                             tb[:, TT:2 * TT], ALU.add)
            mm.inc(NT)
            ev.inc(NT)
    emit_barrier(nc)


def emit_bounds(pr, q_chunks, k_chunks, bones, bsel, cpb):
    nc = pr.nc
    b = pr.bufs
    mm, ev, ac = pr.sems["mm"], pr.sems["ev"], pr.sems["ac"]
    sqt = b["SQT"]
    nrm = b["NRM"]
    with contextlib.ExitStack() as ps_scope:
        ps_n = ps_scope.enter_context(nc.psum_tensor(U("bn"), [16, T], F32))[:]
        ps_t = ps_scope.enter_context(nc.psum_tensor(U("bt"), [1, 16], BF16))[:]
        ps_c = ps_scope.enter_context(nc.psum_tensor(U("bc"), [128, 8], F32))[:]
        for side, chunks in ((0, q_chunks), (1, k_chunks)):
            ac0, mm0 = ac.v, mm.v
            with nc.Block(no_gpsimd_drain=True) as block:
                @block.scalar
                def _(scalar):
                    for ci, ch in enumerate(chunks):
                        if ci >= 1:
                            scalar.wait_ge(mm.h, mm0 + ci)
                        scalar.activation(sqt, ch, AF.Square).then_inc(ac.h, 1)

                @block.tensor
                def _(tensor):
                    for ci in range(len(chunks)):
                        tensor.wait_ge(ac.h, ac0 + ci + 1)
                        for t in range(NT):
                            mmi = tensor.matmul(
                                ps_n[:, t * TT:(t + 1) * TT], bsel(ci),
                                sqt[:, t * TT:(t + 1) * TT],
                                start=(ci == 0), stop=(ci == len(chunks) - 1))
                        mmi.then_inc(mm.h, 1)
            ac.inc(len(chunks))
            mm.inc(len(chunks))
            with nc.Block(no_gpsimd_drain=True) as block:
                @block.vector
                def _(vector):
                    vector.reduce_max(nrm[:, side:side + 1], ps_n, axis=AX.X)
        with nc.Block(no_gpsimd_drain=True) as block:
            @block.vector
            def _(vector):
                vector.tensor_tensor(nrm[:, 2:3], nrm[:, 0:1], nrm[:, 1:2],
                                     ALU.mult).then_inc(ev.h, 1)

            @block.scalar
            def _(scalar):
                scalar.wait_ge(ev.h, ev.inc())
                scalar.activation(nrm[:, 2:3], nrm[:, 2:3], AF.Ln)
                scalar.activation(b["C16B"], nrm[0:16, 2:3], AF.Exp,
                                  scale=0.5, bias=b["CONSTS"][0:16, 1:2])\
                    .then_inc(ac.h, 1)

            @block.tensor
            def _(tensor):
                tensor.wait_ge(ac.h, ac.inc())
                tensor.transpose(ps_t, b["C16B"], b["IDN"][0:16, 0:16])\
                    .then_inc(mm.h, 1)

            @block.vector
            def _(vector):
                vector.wait_ge(mm.h, mm.inc())
                vector.reduce_max(b["ROW8F"],
                                  ps_t.rearrange("a (g two) -> a g two", two=2),
                                  axis=AX.X)
                vector.tensor_scalar(b["ROW8B"], b["ROW8F"], -1.0, None,
                                     ALU.mult).then_inc(ev.h, 1)

            @block.tensor
            def _(tensor):
                tensor.wait_ge(ev.h, ev.inc())
                tensor.matmul(ps_c, b["ONE1"][0:1], b["ROW8B"],
                              start=True, stop=True).then_inc(mm.h, 1)

            @block.scalar
            def _(scalar):
                scalar.wait_ge(mm.h, mm.inc())
                scalar.activation(cpb, ps_c, AF.Copy)
    emit_barrier(nc)

def emit_attn_core(pr, pieces_fn, vsb, attscr, cpb):
    nc = pr.nc
    b = pr.bufs
    mm, ev, ac, st = (pr.sems[s] for s in ("mm", "ev", "ac", "st"))
    ytmp = b["TMP1"]
    wld = pr.sems["wld"]
    with contextlib.ExitStack() as ps_scope:
        masks = ps_scope.enter_context(
            nc.sbuf_tensor(U("MASKS01"), [128, 4 * 2 * TT], BF16))[:]
        pt = ps_scope.enter_context(
            nc.sbuf_tensor(U("PT"), [128, 2 * 1024], BF16))[:]
        srow2 = ps_scope.enter_context(
            nc.sbuf_tensor(U("SROW2"), [1, 2 * TT], F32))[:]
        rbrow = ps_scope.enter_context(
            nc.sbuf_tensor(U("RBROW"), [1, 2 * TT], BF16))[:]
        ato = ps_scope.enter_context(
            nc.sbuf_tensor(U("ATO"), [64, 4 * TT], BF16))[:]
        ps_sc = [ps_scope.enter_context(
            nc.psum_tensor(U(f"at_sc{i}"), [128, 2 * TT], F32))[:]
            for i in range(2)]
        ps_y = [ps_scope.enter_context(
            nc.psum_tensor(U(f"at_y{i}"), [64, TT], F32))[:] for i in range(2)]
        ps_s = [ps_scope.enter_context(
            nc.psum_tensor(U(f"at_s{i}"), [1, TT], F32))[:] for i in range(2)]
        wtm = emit_loads(pr, [
            (masks.rearrange("p (a q) -> p a q", q=2 * TT),
             pr.dram["MASKS01"].rearrange("a p q -> p a q"))])
        st0 = st.v
        first_wait = [True]
        for hg in range(NH // 2):
            for qt in range(NT):
                gi = hg * NT + qt
                kn = 4 * (qt + 1)
                mm0, ac0, ev0 = mm.v, ac.v, ev.v
                with nc.Block(no_gpsimd_drain=True) as block:
                    @block.tensor
                    def _(tensor):
                        if first_wait[0]:
                            tensor.wait_ge(wld.h, wtm[-1])
                            first_wait[0] = False
                        for kc in range(kn):
                            if kc >= 2:
                                tensor.wait_ge(ac.h, ac0 + kc - 1)
                            for hh in range(2):
                                h = 2 * hg + hh
                                allp = list(pieces_fn(h, kc, qt))
                                np_ = len(allp)
                                for j, piece in enumerate(allp):
                                    tp = piece[2] if len(piece) > 2 else None
                                    mmi = tensor.matmul(
                                        ps_sc[kc % 2][:, hh * TT:(hh + 1) * TT],
                                        piece[0], piece[1],
                                        start=(j == 0), stop=(j == np_ - 1),
                                        tile_position=tp)
                            mmi.then_inc(mm.h, 1)
                            if kc >= 4 * qt:
                                tensor.wait_ge(ev.h,
                                               ev0 + (kc - 4 * qt) + 1)
                            else:
                                tensor.wait_ge(ac.h, ac0 + kc + 1)
                            for hh in range(2):
                                h = 2 * hg + hh
                                tensor.matmul(
                                    ps_y[hh],
                                    vsb[:, kc * 1024 + 64 * h:
                                        kc * 1024 + 64 * (h + 1)],
                                    pt[:, (kc % 2) * 1024 + hh * TT:
                                       (kc % 2) * 1024 + (hh + 1) * TT],
                                    start=(kc == 0), stop=False)
                            for hh in range(2):
                                mmi = tensor.matmul(
                                    ps_s[hh],
                                    b["ONEC"],
                                    pt[:, (kc % 2) * 1024 + hh * TT:
                                       (kc % 2) * 1024 + (hh + 1) * TT],
                                    start=(kc == 0), stop=False)
                            mmi.then_inc(mm.h, 1)

                    @block.scalar
                    def _(scalar):
                        for kc in range(kn):
                            scalar.wait_ge(mm.h, mm0 + 2 * kc + 1)
                            scalar.activation(
                                pt[:, (kc % 2) * 1024:(kc % 2 + 1) * 1024],
                                ps_sc[kc % 2], AF.Exp,
                                bias=cpb[:, hg:hg + 1], scale=0.125)\
                                .then_inc(ac.h, 1)

                    @block.vector
                    def _(vector):
                        for kc in range(4 * qt, kn):
                            o = kc - 4 * qt
                            vector.wait_ge(ac.h, ac0 + kc + 1)
                            vector.tensor_tensor(
                                pt[:, (kc % 2) * 1024:(kc % 2 + 1) * 1024],
                                pt[:, (kc % 2) * 1024:(kc % 2 + 1) * 1024],
                                masks[:, o * 2 * TT:(o + 1) * 2 * TT],
                                ALU.mult).then_inc(ev.h, 1)
                mm.inc(2 * kn)
                ac.inc(kn)
                ev.inc(4)
                with nc.Block(no_gpsimd_drain=True) as block:
                    @block.vector
                    def _(vector):
                        vector.tensor_copy(srow2[:, 0:TT], ps_s[0])
                        vector.tensor_copy(srow2[:, TT:2 * TT], ps_s[1])\
                            .then_inc(ev.h, 1)

                    @block.scalar
                    def _(scalar):
                        scalar.wait_ge(ev.h, ev.inc())
                        scalar.activation(srow2, srow2, AF.Ln)
                        scalar.activation(rbrow, srow2, AF.Exp,
                                          scale=-1.0).then_inc(ac.h, 1)

                    @block.tensor
                    def _(tensor):
                        tensor.wait_ge(ac.h, ac.inc())
                        tensor.matmul(ps_sc[0][0:64, 0:TT],
                                      b["ONE1"][0:1, 0:64],
                                      rbrow[:, 0:TT], start=True, stop=True)
                        tensor.matmul(ps_sc[0][0:64, TT:2 * TT],
                                      b["ONE1"][0:1, 0:64],
                                      rbrow[:, TT:2 * TT], start=True,
                                      stop=True).then_inc(mm.h, 1)

                    @block.vector
                    def _(vector):
                        vector.wait_ge(mm.h, mm.inc())
                        if gi >= 2:
                            vector.wait_ge(st.h, st0 + 32 * (gi - 1))
                        for hh in range(2):
                            vector.tensor_copy(ytmp[0:64], ps_y[hh])
                            ins = vector.tensor_tensor(
                                ato[:, ((gi % 2) * 2 + hh) * TT:
                                    ((gi % 2) * 2 + hh + 1) * TT],
                                ytmp[0:64],
                                ps_sc[0][0:64, hh * TT:(hh + 1) * TT],
                                ALU.mult)
                        ins.then_inc(ev.h, 1)

                    @block.sync
                    def _(sync):
                        sync.wait_ge(ev.h, ev.v + 1)
                        for hh in range(2):
                            sync.dma_start(
                                out=attscr[hg * 128 + 64 * hh:
                                           hg * 128 + 64 * (hh + 1),
                                           qt * TT:(qt + 1) * TT],
                                in_=ato[:, ((gi % 2) * 2 + hh) * TT:
                                        ((gi % 2) * 2 + hh + 1) * TT])\
                                .then_inc(st.h, 16)
                ev.inc(1)
                st.inc(32)
        with nc.Block(no_gpsimd_drain=True) as block:
            @block.sync
            def _(sync):
                sync.wait_ge(st.h, st.v)
    emit_barrier(nc)


def emit_outproj(pr, wdram, attscr, dst, mode, ystream=None):
    nc = pr.nc
    ld, wld, ev, mm, st = (pr.sems[s] for s in ("ld", "wld", "ev", "mm", "st"))
    r3 = lambda a: a.rearrange("(c p) n -> p c n", p=128)
    with contextlib.ExitStack() as sc:
        WOW = sc.enter_context(
            nc.sbuf_tensor(U("WOW"), [128, 8 * 1024], BF16))[:]
        ATB = [sc.enter_context(
            nc.sbuf_tensor(U(f"ATB{j}"), [128, 8 * TT], BF16))[:]
            for j in range(2)]
        if mode == "add_stream":
            YTB = [sc.enter_context(
                nc.sbuf_tensor(U(f"YTB{j}"), [128, 8 * TT], BF16))[:]
                for j in range(2)]
        else:
            STG = [sc.enter_context(
                nc.sbuf_tensor(U(f"STG{j}"), [128, TT], F32))[:]
                for j in range(4)]
        pss = [sc.enter_context(
            nc.psum_tensor(U(f"op{i}"), [128, TT], F32))[:] for i in range(2)]
        wt = emit_loads(pr, [
            (WOW.rearrange("p (c n) -> p c n", n=1024), r3(wdram))])
        ld0, ev0, mm0 = ld.v, ev.v, mm.v
        nper = 32 if mode == "add_stream" else 16

        def sync_sec(sync):
            for t in range(NT):
                sync.dma_start(
                    out=ATB[t % 2].rearrange("p (c n) -> p c n", n=TT),
                    in_=attscr[:, t * TT:(t + 1) * TT].rearrange(
                        "(c p) n -> p c n", p=128)).then_inc(ld.h, 16)
                if mode == "add_stream":
                    sync.dma_start(
                        out=YTB[t % 2].rearrange("p (c n) -> p c n", n=TT),
                        in_=ystream[:, t * TT:(t + 1) * TT].rearrange(
                            "(c p) n -> p c n", p=128)).then_inc(ld.h, 16)
                if t < NT - 2:
                    sync.wait_ge(ev.h, ev0 + (t + 1) * NCH)

        with nc.Block(no_gpsimd_drain=True) as block:
            @block.sync
            def _(sync):
                sync_sec(sync)

            @block.tensor
            def _(tensor):
                tensor.wait_ge(wld.h, wt[-1])
                for t in range(NT):
                    tensor.wait_ge(ld.h, ld0 + nper * (t + 1))
                    for fc in range(NCH):
                        i = t * NCH + fc
                        if i >= 2:
                            tensor.wait_ge(ev.h, ev0 + i - 1)
                        for c in range(NCH):
                            mmi = tensor.matmul(
                                pss[i % 2],
                                WOW[:, c * 1024 + fc * 128:
                                    c * 1024 + (fc + 1) * 128],
                                ATB[t % 2][:, c * TT:(c + 1) * TT],
                                start=(c == 0), stop=(c == NCH - 1))
                        mmi.then_inc(mm.h, 1)

            @block.vector
            def _(vector):
                st0v = st.v
                for t in range(NT):
                    for fc in range(NCH):
                        i = t * NCH + fc
                        vector.wait_ge(mm.h, mm0 + i + 1)
                        sl = slice(t * TT, (t + 1) * TT)
                        if mode == "add_stream":
                            vector.tensor_tensor(
                                dst[fc][:, sl],
                                YTB[t % 2][:, fc * TT:(fc + 1) * TT],
                                pss[i % 2], ALU.add).then_inc(ev.h, 1)
                        else:
                            if i >= 4:
                                vector.wait_ge(st.h, st0v + 16 * (i - 3))
                            vector.tensor_copy(STG[i % 4],
                                               pss[i % 2]).then_inc(ev.h, 1)

            if mode == "rmw":
                st0r = st.v
                @block.gpsimd
                def _(gpsimd):
                    for t in range(NT):
                        for fc in range(NCH):
                            i = t * NCH + fc
                            gpsimd.wait_ge(ev.h, ev0 + i + 1)
                            gpsimd.dma_start(
                                out=dst[fc * 128:(fc + 1) * 128,
                                        t * TT:(t + 1) * TT],
                                in_=STG[i % 4], accum_op=ALU.add)\
                                .then_inc(st.h, 16)
                    st.inc(16 * NT * NCH)
                    gpsimd.wait_ge(st.h, st0r + 16 * NT * NCH)
        ld.inc(nper * NT)
        mm.inc(NT * NCH)
        ev.inc(NT * NCH)
    emit_barrier(nc)


def emit_resid_add(pr, acc, xdram):
    nc = pr.nc
    ld, ev = pr.sems["ld"], pr.sems["ev"]
    with contextlib.ExitStack() as sc:
        XT2 = sc.enter_context(
            nc.sbuf_tensor(U("XT2"), [128, 2 * T], F32, side="right"))[:]
        ld0, ev0 = ld.v, ev.v
        with nc.Block(no_gpsimd_drain=True) as block:
            @block.sync
            def _(sync):
                for c in range(NCH):
                    sync.dma_start(out=XT2[:, (c % 2) * T:(c % 2 + 1) * T],
                                   in_=xdram[c * 128:(c + 1) * 128, :])\
                        .then_inc(ld.h, 16)
                    if c < NCH - 2:
                        sync.wait_ge(ev.h, ev0 + c + 1)

            @block.vector
            def _(vector):
                for c in range(NCH):
                    vector.wait_ge(ld.h, ld0 + 16 * (c + 1))
                    vector.tensor_tensor(
                        acc[c], acc[c], XT2[:, (c % 2) * T:(c % 2 + 1) * T],
                        ALU.add).then_inc(ev.h, 1)
        ld.inc(16 * NCH)
        ev.inc(NCH)
    emit_barrier(nc)


def emit_ffn(pr, tag, xsrc, xdst):
    nc = pr.nc
    b = pr.bufs
    dram = pr.dram
    XBF = pr.XBF
    ld, wld, ev, mm, ac, st = (pr.sems[s] for s in
                               ("ld", "wld", "ev", "mm", "ac", "st"))
    r3 = lambda a, p=128: a.rearrange("(c p) n -> p c n", p=p)

    emit_ln(pr, "hbm", xsrc, XBF, None)
    pr.tap(f"y{tag}", XBF)
    if pr.stop_hit:
        return

    s_mla = contextlib.ExitStack()
    s_rn = contextlib.ExitStack()
    L = lambda st_, name, shape, dt: st_.enter_context(
        nc.sbuf_tensor(U(name), list(shape), dt))[:]
    QATT = [L(s_mla, f"QATT{i}", (128, T), BF16) for i in range(NCH)]
    VSB = L(s_mla, "VSB", (128, 16 * 1024), BF16)
    PAB = L(s_mla, "PAB", (128, 4 * 128), BF16)
    ROTQ = [L(s_rn, f"ROTQ{i}", (128, T), BF16) for i in range(4)]
    ROTK = [L(s_rn, f"ROTK{i}", (128, T), BF16) for i in range(4)]
    WKV = VSB[:, 0:2048]
    WQD = VSB[:, 2048:4096]
    WRK = VSB[:, 4096:8192]
    s_lat = contextlib.ExitStack()
    s_w2 = contextlib.ExitStack()
    R = lambda st_, name, shape, dt: st_.enter_context(
        nc.sbuf_tensor(U(name), list(shape), dt, side="right"))[:]
    QL = R(s_lat, "QL", (128, 2 * T), BF16)
    KVL = R(s_lat, "KVL", (128, 2 * T), BF16)
    WNKP = R(s_w2, "WNKP", (128, 2 * 1024), BF16)
    WNQP = R(s_w2, "WNQP", (128, 2 * 1024), BF16)
    CS1 = R(s_w2, "CS1", (128, T), BF16)
    CS2 = R(s_w2, "CS2", (128, T), BF16)
    WRQ = R(s_w2, "WRQ", (128, 2 * 512), BF16)
    wt1 = emit_loads(pr, [
        (WKV.rearrange("p (c n) -> p c n", n=256), r3(dram["kv_dT"])),
        (WQD.rearrange("p (c n) -> p c n", n=256), r3(dram["q_dT"])),
        (WRK.rearrange("p (c n) -> p c n", n=512), r3(dram["rope_kT"])),
        (WNKP.rearrange("p (c n) -> p c n", n=1024), r3(dram["ku_nP"])),
        (WNQP.rearrange("p (c n) -> p c n", n=1024), r3(dram["qu_nP"])),
        (WRQ.rearrange("p (c n) -> p c n", n=512), r3(dram["rope_qT"])),
        (CS1, dram["CS1"]), (CS2, dram["CS2"]),
        (PAB.rearrange("p (a n) -> p a n", n=128),
         dram["PAB"].rearrange("a p n -> p a n")),
    ])

    A1 = QATT
    emit_ln(pr, "sbuf_bf", XBF, A1, b["LN1W"])
    pr.tap(f"a1{tag}", A1)
    if pr.stop_hit:
        return

    jobs = []
    for lc in range(2):
        for t in range(NT):
            sl = slice(t * TT, (t + 1) * TT)
            jobs.append((KVL[:, lc * T:][:, sl],
                         [(WKV[:, c * 256 + lc * 128:c * 256 + (lc + 1) * 128],
                           A1[c][:, sl]) for c in range(NCH)]))
            jobs.append((QL[:, lc * T:][:, sl],
                         [(WQD[:, c * 256 + lc * 128:c * 256 + (lc + 1) * 128],
                           A1[c][:, sl]) for c in range(NCH)]))
    for rc in range(4):
        for t in range(NT):
            sl = slice(t * TT, (t + 1) * TT)
            jobs.append((ROTK[rc][:, sl],
                         [(WRK[:, c * 512 + rc * 128:c * 512 + (rc + 1) * 128],
                           A1[c][:, sl]) for c in range(NCH)]))
    emit_proj(pr, jobs, waits=[(wld.h, wt1[-1])])
    pr.tap(f"kvl{tag}", [KVL, QL])
    if pr.stop_hit:
        return

    jobs = []
    for rc in range(4):
        for t in range(NT):
            sl = slice(t * TT, (t + 1) * TT)
            jobs.append((ROTQ[rc][:, sl],
                         [(WRQ[:, lc * 512 + rc * 128:lc * 512 + (rc + 1) * 128],
                           QL[:, lc * T:][:, sl]) for lc in range(2)]))
    emit_proj(pr, jobs)
    pr.tap(f"rotkpre{tag}", ROTK)
    if pr.stop_hit:
        return

    emit_rotate(pr, ROTK + ROTQ, CS1, CS2, b["SWAP32"], "m")
    pr.tap(f"rotk{tag}", ROTK)
    pr.tap(f"rotq{tag}", ROTQ)
    if pr.stop_hit:
        return

    with nc.Block(no_gpsimd_drain=True) as block:
        @block.sync
        def _(sync):
            for c in range(NCH):
                sync.dma_start(out=pr.dram_yscr[c * 128:(c + 1) * 128, :],
                               in_=XBF[c]).then_inc(st.h, 16)
            st.inc(16 * NCH)
            sync.wait_ge(st.h, st.v)
    emit_barrier(nc)
    KATT = XBF
    jobs = []
    for c in range(NCH):
        var = c % 2
        for t in range(NT):
            sl = slice(t * TT, (t + 1) * TT)
            jobs.append((KATT[c][:, sl],
                         [(PAB[:, var * 128:(var + 1) * 128],
                           ROTK[c // 2][:, sl])] +
                         [(WNKP[:, lc * 1024 + c * 128:lc * 1024 + (c + 1) * 128],
                           KVL[:, lc * T:][:, sl]) for lc in range(2)]))
    emit_proj(pr, jobs)
    jobs = []
    for c in range(NCH):
        var = c % 2
        for t in range(NT):
            sl = slice(t * TT, (t + 1) * TT)
            jobs.append((QATT[c][:, sl],
                         [(PAB[:, var * 128:(var + 1) * 128],
                           ROTQ[c // 2][:, sl])] +
                         [(WNQP[:, lc * 1024 + c * 128:lc * 1024 + (c + 1) * 128],
                           QL[:, lc * T:][:, sl]) for lc in range(2)]))
    emit_proj(pr, jobs)
    s_rn.close()
    pr.tap(f"katt{tag}", KATT)
    pr.tap(f"qatt{tag}", QATT)
    if pr.stop_hit:
        return

    s_vw = contextlib.ExitStack()
    WVU = L(s_vw, "WVU", (128, 2 * 1024), BF16)
    wt_v = emit_loads(pr, [
        (WVU.rearrange("p (c n) -> p c n", n=1024), r3(dram["v_uT"]))])
    jobs = []
    for tc in range(16):
        for vh in range(2):
            jobs.append((VSB[:, tc * 1024 + vh * TT:tc * 1024 + (vh + 1) * TT],
                         [(KVL[:, lc * T + tc * 128:lc * T + (tc + 1) * 128],
                           WVU[:, lc * 1024 + vh * TT:lc * 1024 + (vh + 1) * TT])
                          for lc in range(2)]))
    emit_proj(pr, jobs, waits=[(wld.h, wt_v[-1])])
    s_vw.close()
    pr.tap(f"vsb{tag}", [VSB[:, i * 1024:(i + 1) * 1024] for i in range(16)])
    if pr.stop_hit:
        return
    s_w2.close()
    s_lat.close()

    CPB = b["CPB"]
    emit_bounds(pr, QATT, KATT, b["BONES_D"],
                lambda ci: b["BONES_D"][:, ci * 16:(ci + 1) * 16], CPB)

    def mla_pieces(h, kc, qt):
        cg, rb = h // 2, 64 * (h % 2)
        ksl = slice(kc * 128, (kc + 1) * 128)
        qsl = slice(qt * TT, (qt + 1) * TT)
        return [(KATT[cg][rb:rb + 64, ksl], QATT[cg][rb:rb + 64, qsl],
                 (rb, 0))]

    emit_attn_core(pr, mla_pieces, VSB, pr.dram_attscr, CPB)
    pr.tap(f"atty{tag}",
           [pr.dram_attscr[c * 128:(c + 1) * 128, :] for c in range(NCH)])
    s_mla.close()
    if pr.stop_hit:
        return

    emit_outproj(pr, dram["o_wT"], pr.dram_attscr, XBF, "add_stream",
                 ystream=pr.dram_yscr)
    pr.tap(f"u{tag}", XBF)
    if pr.stop_hit:
        return

    s_xt = contextlib.ExitStack()
    XT2 = s_xt.enter_context(
        nc.sbuf_tensor(U("XT2"), [128, 2 * T], F32, side="right"))[:]
    STG2 = s_xt.enter_context(
        nc.sbuf_tensor(U("STG2"), [128, 2 * T], F32, side="right"))[:]

    def acc_init():
        ld0, st0, ev0 = ld.v, st.v, ev.v
        with nc.Block(no_gpsimd_drain=True) as block:
            @block.sync
            def _(sync):
                for c in range(NCH):
                    sync.dma_start(out=XT2[:, (c % 2) * T:(c % 2 + 1) * T],
                                   in_=xsrc[c * 128:(c + 1) * 128, :])\
                        .then_inc(ld.h, 16)
                    if c < NCH - 2:
                        sync.wait_ge(ev.h, ev0 + c + 1)
                ld.inc(16 * NCH)

            @block.vector
            def _(vector):
                for c in range(NCH):
                    vector.wait_ge(ld.h, ld0 + 16 * (c + 1))
                    if c >= 2:
                        vector.wait_ge(st.h, st0 + 16 * (c - 1))
                    vector.scalar_tensor_tensor(
                        STG2[:, (c % 2) * T:(c % 2 + 1) * T], XBF[c], 0.5,
                        XT2[:, (c % 2) * T:(c % 2 + 1) * T],
                        ALU.mult, ALU.add).then_inc(ev.h, 1)
                ev.inc(NCH)

            @block.gpsimd
            def _(gpsimd):
                for c in range(NCH):
                    gpsimd.wait_ge(ev.h, ev0 + c + 1)
                    gpsimd.dma_start(out=xdst[c * 128:(c + 1) * 128, :],
                                     in_=STG2[:, (c % 2) * T:(c % 2 + 1) * T])\
                        .then_inc(st.h, 16)
                st.inc(16 * NCH)
                gpsimd.wait_ge(st.h, st.v)
        emit_barrier(nc)

    emit_ln(pr, "sbuf_bf", XBF, XBF, b["LN2W"], pre_norm_hook=acc_init)
    s_xt.close()
    A2 = XBF
    pr.tap(f"a2{tag}", A2)
    pr.tap(f"accmid{tag}",
           [xdst[c * 128:(c + 1) * 128, :] for c in range(NCH)])
    if pr.stop_hit:
        return

    s_wr = contextlib.ExitStack()
    b["WROW"] = s_wr.enter_context(
        nc.sbuf_tensor(U("WROW"), [7, T], BF16, side="right"))[:]
    emit_gating(pr, tag, A2)
    if pr.stop_hit:
        return
    emit_moe(pr, tag, A2, xdst)
    s_wr.close()
    pr.tap(f"acc{tag}",
           [xdst[c * 128:(c + 1) * 128, :] for c in range(NCH)])


def emit_gating(pr, tag, a2):
    nc = pr.nc
    b = pr.bufs
    dram = pr.dram
    ld = pr.sems["ld"]
    mm, ev, ac, wld = (pr.sems[s] for s in ("mm", "ev", "ac", "wld"))
    wrow = b["WROW"]
    s_g = contextlib.ExitStack()
    grows_p = s_g.enter_context(nc.sbuf_tensor(U("GPROB"), [7, T], F32))[:]
    grows_s = s_g.enter_context(nc.sbuf_tensor(U("GSCR"), [7, T], F32))[:]
    gsc = b["GSC"]
    gtmp = b["GTMP"]
    WRT = b["WRT"]
    RB7 = b["RB7"]
    wt = emit_loads(pr, [
        (WRT.rearrange("p (c n) -> p c n", n=7),
         dram["router_wT"].rearrange("(c p) n -> p c n", p=128)),
    ])
    ld0 = ld.v
    with nc.Block(no_gpsimd_drain=True) as block:
        @block.sync
        def _(sync):
            sync.dma_start(out=RB7, in_=dram["rbias"]).then_inc(ld.h, 16)
        ld.inc(16)

    with contextlib.ExitStack() as ps_scope:
        ps_l = ps_scope.enter_context(nc.psum_tensor(U("gl"), [7, T], F32))[:]
        mm0 = mm.v
        with nc.Block(no_gpsimd_drain=True) as block:
            @block.tensor
            def _(tensor):
                tensor.wait_ge(wld.h, wt[-1])
                for t in range(NT):
                    for c in range(NCH):
                        mmi = tensor.matmul(
                            ps_l[:, t * TT:(t + 1) * TT],
                            WRT[:, c * 7:(c + 1) * 7],
                            a2[c][:, t * TT:(t + 1) * TT],
                            start=(c == 0), stop=(c == NCH - 1))
                mmi.then_inc(mm.h, 1)

            @block.vector
            def _(vector):
                vector.wait_ge(ld.h, ld0 + 16)
                vector.tensor_scalar(RB7, RB7, -1.0, None, ALU.mult)\
                    .then_inc(ev.h, 1)

            @block.scalar
            def _(scalar):
                scalar.wait_ge(mm.h, mm.inc())
                scalar.wait_ge(ev.h, ev.inc())
                scalar.activation(grows_p, ps_l, AF.Exp, scale=-1.0,
                                  bias=RB7).then_inc(ac.h, 1)

            @block.vector
            def _(vector):
                vector.wait_ge(ac.h, ac.inc())
                vector.tensor_scalar(grows_s, grows_p, 1.0,
                                     None, ALU.add).then_inc(ev.h, 1)

            @block.scalar
            def _(scalar):
                scalar.wait_ge(ev.h, ev.inc())
                scalar.activation(grows_s, grows_s, AF.Ln)
                scalar.activation(grows_p, grows_s, AF.Exp,
                                  scale=-1.0).then_inc(ac.h, 1)
        ac.inc(1)
    emit_barrier(nc)
    pr.tap(f"probs{tag}", [grows_p])

    prtf = b["PRTF"]
    wgtb = b["WGTB"]
    with contextlib.ExitStack() as ps_scope:
        ps_t = [ps_scope.enter_context(
            nc.psum_tensor(U(f"gt{i}"), [128, 8], F32))[:] for i in range(2)]
        mm0, ev0 = mm.v, ev.v
        with nc.Block(no_gpsimd_drain=True) as block:
            @block.tensor
            def _(tensor):
                for tc in range(16):
                    if tc >= 2:
                        tensor.wait_ge(ev.h, ev0 + tc - 1)
                    tensor.transpose(ps_t[tc % 2][:, 0:7],
                                     grows_p[:, tc * 128:(tc + 1) * 128],
                                     b["IDNF"][0:7, 0:7]).then_inc(mm.h, 1)

            @block.vector
            def _(vector):
                for tc in range(16):
                    vector.wait_ge(mm.h, mm0 + tc + 1)
                    vector.tensor_copy(prtf[:, tc * 8:tc * 8 + 7],
                                       ps_t[tc % 2][:, 0:7]).then_inc(ev.h, 1)
        mm.inc(16)
        ev.inc(16)
    emit_barrier(nc)
    pr.tap(f"prtf{tag}", [prtf])

    gmsk = b["GMSK"]
    bneg = b["BNEG"]
    bzero = b["BZERO"]
    ev0, ac0 = ev.v, ac.v
    with nc.Block(no_gpsimd_drain=True) as block:
        @block.vector
        def _(vector):
            gg = pr.sems["gg"]
            for tc in range(16):
                p7 = prtf[:, tc * 8:tc * 8 + 7]
                vector.reduce_max(gsc[:, 0:1], p7, axis=AX.X)\
                    .then_inc(gg.h, 1)
                vector.wait_ge(gg.h, gg.inc())
                vector.tensor_scalar(gmsk[:, 0:7], p7, gsc[:, 0:1], None,
                                     ALU.is_equal)
                vector.select(gtmp[:, 0:7], gmsk[:, 0:7], bneg[:, 0:7], p7)\
                    .then_inc(gg.h, 1)
                vector.wait_ge(gg.h, gg.inc())
                vector.reduce_max(gsc[:, 1:2], gtmp[:, 0:7], axis=AX.X)\
                    .then_inc(gg.h, 1)
                vector.wait_ge(gg.h, gg.inc())
                vector.tensor_scalar(gmsk[:, 0:7], p7, gsc[:, 1:2], None,
                                     ALU.is_ge)
                vector.select(gtmp[:, 8:15], gmsk[:, 0:7], p7, bzero[:, 0:7])\
                    .then_inc(gg.h, 1)
                vector.wait_ge(gg.h, gg.inc())
                vector.reduce_sum(gsc[:, 2:3], gtmp[:, 8:15], axis=AX.X)\
                    .then_inc(ev.h, 1)
                vector.wait_ge(ac.h, ac0 + tc + 1)
                vector.tensor_scalar(wgtb[:, tc * 8:tc * 8 + 7],
                                     gtmp[:, 8:15], gsc[:, 3:4], 0.5,
                                     ALU.mult, ALU.mult).then_inc(ev.h, 1)

        @block.scalar
        def _(scalar):
            for tc in range(16):
                scalar.wait_ge(ev.h, ev0 + 2 * tc + 1)
                scalar.activation(gsc[:, 3:4], gsc[:, 2:3], AF.Ln)
                scalar.activation(gsc[:, 3:4], gsc[:, 3:4], AF.Exp,
                                  scale=-1.0).then_inc(ac.h, 1)
    ev.inc(32)
    ac.inc(16)
    emit_barrier(nc)
    pr.tap(f"wgtb{tag}", [wgtb])

    with contextlib.ExitStack() as ps_scope:
        ps_w = [ps_scope.enter_context(
            nc.psum_tensor(U(f"gw{i}"), [16, 128], BF16))[:] for i in range(2)]
        mm0, ev0 = mm.v, ev.v
        with nc.Block(no_gpsimd_drain=True) as block:
            @block.tensor
            def _(tensor):
                for tc in range(16):
                    if tc >= 2:
                        tensor.wait_ge(ev.h, ev0 + tc - 1)
                    tensor.transpose(ps_w[tc % 2][0:8],
                                     wgtb[:, tc * 8:(tc + 1) * 8],
                                     b["IDN"]).then_inc(mm.h, 1)

            @block.vector
            def _(vector):
                for tc in range(16):
                    vector.wait_ge(mm.h, mm0 + tc + 1)
                    vector.tensor_copy(wrow[:, tc * 128:(tc + 1) * 128],
                                       ps_w[tc % 2][0:7]).then_inc(ev.h, 1)
        mm.inc(16)
        ev.inc(16)
    s_g.close()
    emit_barrier(nc)
    pr.tap(f"wrow{tag}", [wrow])

def emit_moe(pr, tag, a2, xdst):
    nc = pr.nc
    b = pr.bufs
    dram = pr.dram
    mm, ev, ac, wld, st = (pr.sems[s] for s in ("mm", "ev", "ac", "wld", "st"))
    wrow = b["WROW"]
    NIB = 4
    with contextlib.ExitStack() as s_moe:
        HB = [s_moe.enter_context(
            nc.sbuf_tensor(U(f"HB{i}"), [128, T], BF16))[:] for i in range(8)]
        WGB = [s_moe.enter_context(
            nc.sbuf_tensor(U(f"WGB{j}"), [128, 8 * 256], BF16))[:]
            for j in range(2)]
        WUB = [s_moe.enter_context(
            nc.sbuf_tensor(U(f"WUB{j}"), [128, 8 * 256], BF16))[:]
            for j in range(2)]
        WDB = [s_moe.enter_context(
            nc.sbuf_tensor(U(f"WDB{j}"), [128, 8 * 128], BF16))[:]
            for j in range(2)]
        WSB = s_moe.enter_context(nc.sbuf_tensor(U("WSB"), [128, T], BF16))[:]
        SGT = s_moe.enter_context(nc.sbuf_tensor(U("SGT"), [128, 2 * TT], BF16))[:]
        UST = s_moe.enter_context(nc.sbuf_tensor(U("UST"), [128, 2 * TT], BF16))[:]
        STG = [s_moe.enter_context(
            nc.sbuf_tensor(U(f"STG{j}"), [128, T], F32))[:] for j in range(2)]
        ps_g = [s_moe.enter_context(
            nc.psum_tensor(U(f"mg{i}"), [128, TT], F32))[:] for i in range(2)]
        ps_u = [s_moe.enter_context(
            nc.psum_tensor(U(f"mu{i}"), [128, TT], F32))[:] for i in range(2)]
        ps_d = s_moe.enter_context(
            nc.psum_tensor(U("md"), [128, 4 * TT], F32))
        ps_w = ps_g[0]

        r3 = lambda a: a.rearrange("(c p) n -> p c n", p=128)

        gu_tgt = {}
        d_tgt = {}
        wt = emit_loads(pr, [
            (WGB[0].rearrange("p (c n) -> p c n", n=256),
             r3(dram["gT"][0][:, 0:256])),
            (WUB[0].rearrange("p (c n) -> p c n", n=256),
             r3(dram["uT"][0][:, 0:256])),
        ])
        gu_tgt[0] = wt[-1]
        st_moe0 = st.v

        for e in range(8):
            mm0, ac0 = mm.v, ac.v
            if e > 0:
                with nc.Block(no_gpsimd_drain=True) as block:
                    @block.tensor
                    def _(tensor):
                        for t in range(NT):
                            if t >= 1:
                                tensor.wait_ge(ac.h, ac0 + t)
                            tensor.matmul(ps_w,
                                          b["SEL7"][:, (e - 1) * 128:e * 128],
                                          wrow[0:7, t * TT:(t + 1) * TT],
                                          start=True, stop=True)\
                                .then_inc(mm.h, 1)

                    @block.scalar
                    def _(scalar):
                        for t in range(NT):
                            scalar.wait_ge(mm.h, mm0 + t + 1)
                            scalar.activation(WSB[:, t * TT:(t + 1) * TT],
                                              ps_w, AF.Copy).then_inc(ac.h, 1)
                mm.inc(NT)
                ac.inc(NT)

            for half in range(2):
                hb = 2 * e + half
                for ib in range(NIB):
                    glb = 8 * e + 4 * half + ib
                    nxt = glb + 1 if glb < 63 else None
                    loads = []
                    if nxt is not None:
                        ee, bb = nxt // 8, nxt % 8
                        loads += [
                            (WGB[(ib + 1) % 2].rearrange(
                                "p (c n) -> p c n", n=256),
                             r3(dram["gT"][ee][:, bb * 256:(bb + 1) * 256])),
                            (WUB[(ib + 1) % 2].rearrange(
                                "p (c n) -> p c n", n=256),
                             r3(dram["uT"][ee][:, bb * 256:(bb + 1) * 256])),
                        ]
                    if ib == 1:
                        loads.append(
                            (WDB[0].rearrange("p (c n) -> p c n", n=128),
                             dram["dT"][e][half * 1024:(half + 1) * 1024,
                                           0:128].rearrange(
                                 "(c p) n -> p c n", p=128)))
                    mm0, ac0, ev0 = mm.v, ac.v, ev.v
                    wld0 = wld.v
                    with nc.Block(no_gpsimd_drain=True) as block:
                        @block.gpsimd
                        def _(gpsimd):
                            if loads:
                                for dst, src in loads:
                                    gpsimd.dma_start(out=dst, in_=src)\
                                        .then_inc(wld.h, 16)
                                wld.inc(16 * len(loads))
                                if nxt is not None:
                                    gu_tgt[nxt] = wld0 + 32
                                if ib == 1:
                                    d_tgt[(hb, 0)] = wld.v

                        @block.tensor
                        def _(tensor):
                            tensor.wait_ge(wld.h, gu_tgt[glb])
                            for ic2 in range(2):
                                for t in range(NT):
                                    i = ic2 * NT + t
                                    sl = slice(t * TT, (t + 1) * TT)
                                    if i >= 2:
                                        tensor.wait_ge(ac.h, ac0 + i - 1)
                                        tensor.wait_ge(ev.h, ev0 + 2 * i - 3)
                                    for c in range(NCH):
                                        mmi = tensor.matmul(
                                            ps_g[i % 2],
                                            WGB[ib % 2][:, c * 256 + ic2 * 128:
                                                        c * 256 + (ic2 + 1) * 128],
                                            a2[c][:, sl],
                                            start=(c == 0), stop=(c == NCH - 1))
                                    mmi.then_inc(mm.h, 1)
                                    for c in range(NCH):
                                        mmi = tensor.matmul(
                                            ps_u[i % 2],
                                            WUB[ib % 2][:, c * 256 + ic2 * 128:
                                                        c * 256 + (ic2 + 1) * 128],
                                            a2[c][:, sl],
                                            start=(c == 0), stop=(c == NCH - 1))
                                    mmi.then_inc(mm.h, 1)

                        @block.scalar
                        def _(scalar):
                            for i in range(8):
                                scalar.wait_ge(mm.h, mm0 + 2 * i + 1)
                                if i >= 2:
                                    scalar.wait_ge(ev.h, ev0 + 2 * i - 2)
                                scalar.activation(
                                    SGT[:, (i % 2) * TT:(i % 2 + 1) * TT],
                                    ps_g[i % 2], AF.Silu).then_inc(ac.h, 1)

                        @block.vector
                        def _(vector):
                            for ic2 in range(2):
                                for t in range(NT):
                                    i = ic2 * NT + t
                                    ic = ib * 2 + ic2
                                    sl = slice(t * TT, (t + 1) * TT)
                                    vector.wait_ge(mm.h, mm0 + 2 * i + 2)
                                    if e == 0:
                                        vector.tensor_scalar(
                                            UST[:, (i % 2) * TT:(i % 2 + 1) * TT],
                                            ps_u[i % 2], 0.5, None, ALU.mult)\
                                            .then_inc(ev.h, 1)
                                    else:
                                        vector.tensor_tensor(
                                            UST[:, (i % 2) * TT:(i % 2 + 1) * TT],
                                            ps_u[i % 2], WSB[:, sl], ALU.mult)\
                                            .then_inc(ev.h, 1)
                                    vector.wait_ge(ac.h, ac0 + i + 1)
                                    vector.tensor_tensor(
                                        HB[ic][:, sl],
                                        SGT[:, (i % 2) * TT:(i % 2 + 1) * TT],
                                        UST[:, (i % 2) * TT:(i % 2 + 1) * TT],
                                        ALU.mult).then_inc(ev.h, 1)
                    mm.inc(16)
                    ac.inc(8)
                    ev.inc(16)

                for fc in range(NCH):
                    gidx = hb * NCH + fc
                    loads = []
                    if fc < NCH - 1:
                        loads.append(
                            (WDB[(fc + 1) % 2].rearrange(
                                "p (c n) -> p c n", n=128),
                             dram["dT"][e][half * 1024:(half + 1) * 1024,
                                           (fc + 1) * 128:(fc + 2) * 128]
                             .rearrange("(c p) n -> p c n", p=128)))
                    mm0, ev0 = mm.v, ev.v
                    with nc.Block(no_gpsimd_drain=True) as block:
                        @block.tensor
                        def _(tensor):
                            tensor.wait_ge(wld.h, d_tgt[(hb, fc)])
                            for th in range(2):
                                if fc > 0 or th > 0:
                                    tensor.wait_ge(ev.h, ev0 - 1 + th)
                                for ic in range(8):
                                    for t2 in range(2):
                                        t = th * 2 + t2
                                        mmi = tensor.matmul(
                                            ps_d[:, t * TT:(t + 1) * TT],
                                            WDB[fc % 2][:, ic * 128:
                                                        (ic + 1) * 128],
                                            HB[ic][:, t * TT:(t + 1) * TT],
                                            start=(ic == 0), stop=(ic == 7))
                                mmi.then_inc(mm.h, 1)

                        @block.vector
                        def _(vector):
                            for th in range(2):
                                vector.wait_ge(mm.h, mm0 + th + 1)
                                if gidx >= 2 and th == 0:
                                    vector.wait_ge(st.h,
                                                   st_moe0 + 16 * (gidx - 1))
                                vector.tensor_copy(
                                    STG[gidx % 2][:, th * 2 * TT:
                                                  (th + 1) * 2 * TT],
                                    ps_d[:, th * 2 * TT:(th + 1) * 2 * TT])\
                                    .then_inc(ev.h, 1)

                        @block.gpsimd
                        def _(gpsimd):
                            if loads:
                                for dst, src in loads:
                                    gpsimd.dma_start(out=dst, in_=src)\
                                        .then_inc(wld.h, 16)
                                wld.inc(16)
                                d_tgt[(hb, fc + 1)] = wld.v
                            gpsimd.wait_ge(ev.h, ev0 + 2)
                            gpsimd.dma_start(
                                out=xdst[fc * 128:(fc + 1) * 128, :],
                                in_=STG[gidx % 2], accum_op=ALU.add)\
                                .then_inc(st.h, 16)
                            st.inc(16)
                    mm.inc(2)
                    ev.inc(2)
        with nc.Block(no_gpsimd_drain=True) as block:
            @block.sync
            def _(sync):
                sync.wait_ge(st.h, st.v)
    emit_barrier(nc)


def emit_dense(pr, xscr):
    nc = pr.nc
    b = pr.bufs
    dram = pr.dram
    XBF = pr.XBF
    ld, wld, ev, mm, ac = (pr.sems[s] for s in ("ld", "wld", "ev", "mm", "ac"))
    r3 = lambda a: a.rearrange("(c p) n -> p c n", p=128)
    R = lambda st_, name, shape, dt: st_.enter_context(
        nc.sbuf_tensor(U(name), list(shape), dt, side="right"))[:]
    s_att = contextlib.ExitStack()
    L = lambda name, shape, dt: s_att.enter_context(
        nc.sbuf_tensor(U(name), list(shape), dt))[:]
    QPRE = [L(f"QPREd{i}", (128, T), BF16) for i in range(NCH)]
    KPRE = [L(f"KPREd{i}", (128, T), BF16) for i in range(NCH)]

    s_qk = contextlib.ExitStack()
    WQK = R(s_qk, "WQKd", (128, 8 * 2048), BF16)
    wt_qk = emit_loads(pr, [
        (WQK.rearrange("p (c n) -> p c n", n=2048),
         r3(dram["w_qkv_p"][:, 0:2048]))])
    jobs = []
    for oc in range(NCH):
        for t in range(NT):
            sl = slice(t * TT, (t + 1) * TT)
            jobs.append((QPRE[oc][:, sl],
                         [(WQK[:, c * 2048 + oc * 128:c * 2048 + (oc + 1) * 128],
                           XBF[c][:, sl]) for c in range(NCH)]))
    for oc in range(NCH):
        for t in range(NT):
            sl = slice(t * TT, (t + 1) * TT)
            jobs.append((KPRE[oc][:, sl],
                         [(WQK[:, c * 2048 + 1024 + oc * 128:
                               c * 2048 + 1024 + (oc + 1) * 128],
                           XBF[c][:, sl]) for c in range(NCH)]))
    emit_proj(pr, jobs, waits=[(wld.h, wt_qk[-1])])
    s_qk.close()
    pr.tap("qpred", QPRE)
    if pr.stop_hit:
        return

    s_rt = contextlib.ExitStack()
    DC0 = R(s_rt, "DC0d", (128, T), F32)
    SGN = R(s_rt, "SGNd", (128, 1), F32)
    SEL = R(s_rt, "SELd", (16, 8 * 128), F32)
    LSC = R(s_rt, "LSCd", (16, 1), F32)
    SC16 = R(s_rt, "SC16d", (16, 1), F32)
    SRWD = R(s_rt, "SRWDd", (128, 1), F32)
    TMPF = R(s_rt, "TMPFd", (128, 2 * T), F32)
    COSB = R(s_rt, "COSBd", (128, T), BF16)
    SS2B = R(s_rt, "SS2Bd", (128, T), BF16)
    wt_rt = emit_loads(pr, [
        (DC0, dram["DCONST0"]), (SGN, dram["SGN"]),
        (SEL.rearrange("p (c n) -> p c n", n=128),
         dram["SELS"].rearrange("c p n -> p c n")),
        (LSC, dram["log_scale_c"]),
    ])
    with nc.Block(no_gpsimd_drain=True) as block:
        @block.scalar
        def _(scalar):
            scalar.wait_ge(wld.h, wt_rt[-1])
            scalar.activation(SC16, LSC, AF.Exp).then_inc(ac.h, 1)
    ac.inc(1)
    emit_barrier(nc)

    with contextlib.ExitStack() as ps_scope:
        ps_sr = ps_scope.enter_context(
            nc.psum_tensor(U("dsr"), [128, 1], F32))[:]
        for c in range(NCH):
            with nc.Block(no_gpsimd_drain=True) as block:
                @block.tensor
                def _(tensor):
                    tensor.matmul(ps_sr, SEL[:, c * 128:(c + 1) * 128], SC16,
                                  start=True, stop=True).then_inc(mm.h, 1)

                @block.vector
                def _(vector):
                    gg = pr.sems["gg"]
                    vector.wait_ge(mm.h, mm.inc())
                    vector.tensor_copy(SRWD, ps_sr).then_inc(gg.h, 1)
                    vector.wait_ge(gg.h, gg.inc())
                    vector.engine_nop().then_inc(ev.h, 1)
                    u = TMPF[:, 0:T]
                    r = TMPF[:, T:2 * T]
                    vector.tensor_scalar(u, DC0, SRWD, None, ALU.mult)
                    vector.tensor_scalar(r, u, float(2 ** 23), None, ALU.add)
                    vector.tensor_scalar(r, r, float(-(2 ** 23)), None, ALU.add)
                    vector.tensor_tensor(r, u, r, ALU.subtract)\
                        .then_inc(ev.h, 1)

                @block.scalar
                def _(scalar):
                    scalar.wait_ge(ev.h, ev.v + 2)
                    scalar.activation(SS2B, TMPF[:, T:2 * T], AF.Sin,
                                      scale=2 * math.pi).then_inc(ac.h, 1)

                @block.vector
                def _(vector):
                    vector.wait_ge(ac.h, ac.v + 1)
                    u = TMPF[:, 0:T]
                    r = TMPF[:, T:2 * T]
                    vector.tensor_scalar(u, u, 0.25, None, ALU.add)
                    vector.tensor_scalar(r, u, float(2 ** 23), None, ALU.add)
                    vector.tensor_scalar(r, r, float(-(2 ** 23)), None, ALU.add)
                    vector.tensor_tensor(r, u, r, ALU.subtract)\
                        .then_inc(ev.h, 1)

                @block.scalar
                def _(scalar):
                    scalar.wait_ge(ev.h, ev.v + 3)
                    scalar.activation(COSB, TMPF[:, T:2 * T], AF.Sin,
                                      scale=2 * math.pi).then_inc(ac.h, 1)

                @block.vector
                def _(vector):
                    vector.wait_ge(ac.h, ac.v + 2)
                    vector.tensor_scalar(SS2B, SS2B, SGN, None, ALU.mult)
            ev.inc(3)
            ac.inc(2)
            emit_rotate(pr, [QPRE[c], KPRE[c]], COSB, SS2B, b["SWAP64"],
                        f"d{c}")
    s_rt.close()
    pr.tap("qrotd", QPRE)
    pr.tap("krotd", KPRE)
    if pr.stop_hit:
        return

    VSB = L("VSBd", (128, 16 * 1024), BF16)
    s_v = contextlib.ExitStack()
    WVD = R(s_v, "WVDd", (128, 8 * 256), BF16)
    for q4 in range(4):
        wt_v = emit_loads(pr, [
            (WVD.rearrange("p (c n) -> p c n", n=256),
             r3(dram["w_qkv_p"][:, 2048 + q4 * 256:2048 + (q4 + 1) * 256]))])
        jobs = []
        for tc in range(16):
            for vh in range(2):
                vq = q4 * 256 + vh * 128
                jobs.append((VSB[:, tc * 1024 + vq:tc * 1024 + vq + 128],
                             [(XBF[c][:, tc * 128:(tc + 1) * 128],
                               WVD[:, c * 256 + vh * 128:
                                   c * 256 + (vh + 1) * 128])
                              for c in range(NCH)]))
        emit_proj(pr, jobs, waits=[(wld.h, wt_v[-1])])
    s_v.close()
    pr.tap("vsbd", [VSB[:, i * 1024:(i + 1) * 1024] for i in range(16)])
    if pr.stop_hit:
        return

    CPB = b["CPB"]
    emit_bounds(pr, QPRE, KPRE, b["BONES_D"],
                lambda ci: b["BONES_D"][:, ci * 16:(ci + 1) * 16], CPB)

    def dense_pieces(h, kc, qt):
        cg, rb = h // 2, 64 * (h % 2)
        ksl = slice(kc * 128, (kc + 1) * 128)
        qsl = slice(qt * TT, (qt + 1) * TT)
        return [(KPRE[cg][rb:rb + 64, ksl], QPRE[cg][rb:rb + 64, qsl],
                 (rb, 0))]

    emit_attn_core(pr, dense_pieces, VSB, pr.dram_attscr, CPB)
    pr.tap("attyd",
           [pr.dram_attscr[c * 128:(c + 1) * 128, :] for c in range(NCH)])
    s_att.close()
    if pr.stop_hit:
        return

    emit_outproj(pr, dram["w_o"], pr.dram_attscr, xscr, "rmw")
    pr.tap("accB", [xscr[c * 128:(c + 1) * 128, :] for c in range(NCH)])


def build_program(want_taps=(), stop_after=None):
    nc = bass.Bass("TRN2", target_bir_lowering=False)
    pr = Prog(nc, want_taps)
    pr.stop_after = stop_after
    dram = pr.dram

    def din(name, shape, dt=F32):
        dram[name] = nc.dram_tensor(name, list(shape), dt,
                                    kind="ExternalInput").ap()
        return dram[name]

    xT = din("xT", (D, T))
    din("kv_dT", (D, 256)); din("q_dT", (D, 256))
    din("ku_nP", (256, 1024)); din("qu_nP", (256, 1024))
    din("v_uT", (256, D)); din("rope_qT", (256, 512))
    din("rope_kT", (D, 512)); din("o_wT", (D, D))
    din("ln1w", (1, D)); din("ln2w", (1, D))
    din("w_qkv_p", (D, 3 * D)); din("w_o", (D, D))
    din("log_scale_c", (NH, 1))
    din("router_wT", (D, 7)); din("rbias", (7, 1))
    din("gT", (8, D, 2 * D)); din("uT", (8, D, 2 * D)); din("dT", (8, 2 * D, D))
    din("IDN", (128, 128), BF16); din("IDNF", (128, 128), F32)
    din("SWAP32", (128, 128), BF16); din("SWAP64", (128, 128), BF16)
    din("MASKS", (4, 128, TT), BF16)
    din("MASKS01", (4, 128, 2 * TT), BF16)
    din("BONES_M", (4, 128, 16), BF16); din("BONES_D", (8, 128, 16), BF16)
    din("PAB", (4, 128, 128), BF16)
    din("CS1", (128, T), BF16); din("CS2", (128, T), BF16)
    din("DCONST0", (128, T)); din("SGN", (128, 1)); din("SELS", (8, 16, 128))
    din("SEL7", (7, 7 * 128), BF16)
    outT = nc.dram_tensor("outT", [D, T], F32, kind="ExternalOutput").ap()
    xscr = nc.dram_tensor("xscr", [D, T], F32, kind="Internal").ap()
    pr.dram_attscr = nc.dram_tensor("attscr", [D, T], BF16, kind="Internal").ap()
    pr.dram_yscr = nc.dram_tensor("yscr", [D, T], BF16, kind="Internal").ap()

    res = contextlib.ExitStack()
    sb = lambda name, shape, dt: res.enter_context(
        nc.sbuf_tensor(name, list(shape), dt))

    for sname in ("ld", "wld", "ev", "mm", "ac", "st", "gg"):
        pr.sems[sname] = Sem(res.enter_context(nc.semaphore(sname)))
    b = pr.bufs
    b["ONE1"] = sb("ONE1", (128, 128), BF16)[:]
    b["ONEC"] = sb("ONEC", (128, 1), BF16)[:]
    for nme in ("IDN", "SWAP32", "SWAP64"):
        b[nme] = sb(nme + "_sb", (128, 128), BF16)[:]
    b["IDNF"] = sb("IDNF_sb", (128, 128), F32)[:]
    b["BONES_M"] = sb("BONES_M_sb", (128, 4 * 16), BF16)
    b["BONES_D"] = sb("BONES_D_sb", (128, 8 * 16), BF16)
    b["LN1W"] = sb("LN1W", (1, D), BF16)[:]
    b["LN2W"] = sb("LN2W", (1, D), BF16)[:]
    b["SQT"] = sb("SQT", (128, T), BF16)[:]
    b["ROWA"] = sb("ROWA", (1, T), F32)[:]
    b["ROWB"] = sb("ROWB", (1, T), F32)[:]
    b["RR"] = sb("RR", (1, T), BF16)[:]
    b["RN"] = sb("RN", (1, T), BF16)[:]
    b["SEL7"] = sb("SEL7_sb", (7, 7 * 128), BF16)[:]
    b["TMP1"] = sb("TMP1", (128, TT), F32)[:]
    b["CONSTS"] = sb("CONSTS", (128, 4), F32)[:]
    b["NRM"] = sb("NRM", (16, 8), F32)[:]
    b["C16B"] = sb("C16B", (16, 1), BF16)[:]
    b["ROW8F"] = sb("ROW8F", (1, 8), F32)[:]
    b["ROW8B"] = sb("ROW8B", (1, 8), BF16)[:]
    b["CPB"] = sb("CPB", (128, 8), F32)[:]
    b["PRTF"] = sb("PRTF", (128, 128), F32)[:]
    b["WGTB"] = sb("WGTB", (128, 128), BF16)[:]
    b["GSC"] = sb("GSC", (128, 8), F32)[:]
    b["GMSK"] = sb("GMSK", (128, 8), mybir.dt.int32)[:]
    b["BNEG"] = sb("BNEG", (128, 8), F32)[:]
    b["BZERO"] = sb("BZERO", (128, 8), F32)[:]
    b["GTMP"] = sb("GTMP", (128, 16), F32)[:]
    b["WRT"] = sb("WRT", (128, 8 * 7), BF16)[:]
    b["RB7"] = sb("RB7", (7, 1), F32)[:]

    xbf_t = [sb(f"XBF{c}", (128, T), BF16) for c in range(NCH)]
    XBF = [h[:] for h in xbf_t]
    pr.XBF = XBF

    with nc.Block(no_gpsimd_drain=True) as block:
        @block.gpsimd
        def _(gpsimd):
            ld = pr.sems["ld"]

            def load(dst, src):
                gpsimd.dma_start(out=dst, in_=src).then_inc(ld.h, 16)
                ld.inc(16)
            load(b["IDN"], dram["IDN"])
            load(b["IDNF"], dram["IDNF"])
            load(b["SWAP32"], dram["SWAP32"])
            load(b["SWAP64"], dram["SWAP64"])
            load(b["BONES_M"][:].rearrange("p (a h) -> p a h", h=16),
                 dram["BONES_M"].rearrange("a p h -> p a h"))
            load(b["BONES_D"][:].rearrange("p (a h) -> p a h", h=16),
                 dram["BONES_D"].rearrange("a p h -> p a h"))
            load(b["SEL7"], dram["SEL7"])
            load(b["LN1W"], dram["ln1w"])
            load(b["LN2W"], dram["ln2w"])
            gpsimd.wait_ge(ld.h, ld.v)

        @block.vector
        def _(vector):
            vector.memset(b["ONE1"], 1.0)
            vector.memset(b["ONEC"], 1.0)
            vector.memset(b["WGTB"], 0.0)
            vector.memset(b["BNEG"], -1e9)
            vector.memset(b["BZERO"], 0.0)
            vector.memset(b["CONSTS"][:, 0:1], EPS)
            vector.memset(b["CONSTS"][:, 1:2], math.log(0.125))
    emit_barrier(nc)

    def finish():
        with nc.Block(no_gpsimd_drain=True) as block:
            @block.sync
            def _(sync):
                sync.nop()
        pr._res = res
        return nc, pr

    emit_ffn(pr, "A", xT, xscr)
    if pr.stop_hit:
        return finish()

    emit_ln(pr, "hbm", xscr, XBF, None)
    pr.tap("yB", XBF)
    if pr.stop_hit:
        return finish()
    emit_dense(pr, xscr)
    if pr.stop_hit:
        return finish()

    emit_ffn(pr, "C", xscr, outT)
    pr._res = res
    return nc, pr


def emit_store_x(pr, acc, dst):
    nc = pr.nc
    st = pr.sems["st"]
    with nc.Block(no_gpsimd_drain=True) as block:
        @block.sync
        def _(sync):
            for c in range(NCH):
                sync.dma_start(out=dst[c * 128:(c + 1) * 128, :],
                               in_=acc[c]).then_inc(st.h, 16)
            st.inc(16 * NCH)
            sync.wait_ge(st.h, st.v)
    emit_barrier(nc)


_BUILD_CACHE = {}


def _get_program(want_taps=(), stop_after=None):
    key = (tuple(sorted(want_taps)), stop_after)
    if key not in _BUILD_CACHE:
        _BUILD_CACHE[key] = build_program(want_taps, stop_after)
    return _BUILD_CACHE[key]


def _run(inputs, want_taps=(), stop_after=None):
    consts = _host_consts()
    w = _prep_weights(inputs)
    x = np.asarray(inputs["L0_attn_x"], dtype=np.float32)
    nc, pr = _get_program(want_taps, stop_after)
    common = dict(w)
    for k, v in consts.items():
        common[k] = np.ascontiguousarray(v)
    in_maps = []
    for bb in range(x.shape[0]):
        m = dict(common)
        m["xT"] = np.ascontiguousarray(x[bb].T)
        in_maps.append(m)
    res = run_bass_kernel_spmd(nc, in_maps, core_ids=list(range(len(in_maps))))
    return res, pr


def kernel(**inputs):
    res, pr = _run(inputs)
    out = np.stack([np.ascontiguousarray(res.results[bb]["outT"].T)
                    for bb in range(len(res.results))], axis=0)
    return out


# version 05
# speedup vs baseline: 1.2220x; 1.2220x over previous
import math
import contextlib
import numpy as np
import ml_dtypes
import concourse.bass as bass
import concourse.mybir as mybir
from concourse.bass_utils import run_bass_kernel_spmd

F32 = mybir.dt.float32
BF16 = mybir.dt.bfloat16
AF = mybir.ActivationFunctionType
ALU = mybir.AluOpType
AX = mybir.AxisListType

T = 2048
D = 1024
NCH = 8
NT = 4
TT = 512
NH = 16
HD = 64
EPS = 1e-5
MASKVAL = -98304.0


def _host_consts():
    c = {}
    c["IDN"] = np.eye(128, dtype=ml_dtypes.bfloat16)
    c["IDNF"] = np.eye(128, dtype=np.float32)
    s32 = np.zeros((128, 128), dtype=ml_dtypes.bfloat16)
    for b in range(4):
        for i in range(16):
            s32[b * 32 + i, b * 32 + 16 + i] = 1
            s32[b * 32 + 16 + i, b * 32 + i] = 1
    c["SWAP32"] = s32
    s64 = np.zeros((128, 128), dtype=ml_dtypes.bfloat16)
    for b in range(2):
        for i in range(32):
            s64[b * 64 + i, b * 64 + 32 + i] = 1
            s64[b * 64 + 32 + i, b * 64 + i] = 1
    c["SWAP64"] = s64
    m = np.zeros((4, 128, TT), dtype=ml_dtypes.bfloat16)
    kk = np.arange(128)[:, None]
    qq = np.arange(TT)[None, :]
    for i, o in enumerate((0, 128, 256, 384)):
        m[i] = np.where(kk + o > qq, MASKVAL, 0.0).astype(ml_dtypes.bfloat16)
    c["MASKS"] = m
    m01 = np.zeros((4, 128, 2 * TT), dtype=ml_dtypes.bfloat16)
    for i, o in enumerate((0, 128, 256, 384)):
        keep = (kk + o <= qq).astype(ml_dtypes.bfloat16)
        m01[i, :, 0:TT] = keep
        m01[i, :, TT:2 * TT] = keep
    c["MASKS01"] = m01
    bm = np.zeros((4, 128, 16), dtype=ml_dtypes.bfloat16)
    for cg in range(4):
        for p in range(128):
            bm[cg, p, 4 * cg + p // 32] = 1
    c["BONES_M"] = bm
    pab = np.zeros((4, 128, 128), dtype=ml_dtypes.bfloat16)
    for var in range(2):
        for i in range(32):
            pab[var, 64 * var + i, i] = 1
            pab[var, 64 * var + 32 + i, 64 + i] = 1
            pab[2 + var, 64 * var + i, 32 + i] = 1
            pab[2 + var, 64 * var + 32 + i, 96 + i] = 1
    c["PAB"] = pab
    bd = np.zeros((8, 128, 16), dtype=ml_dtypes.bfloat16)
    for ch in range(8):
        for p in range(128):
            bd[ch, p, 2 * ch + p // 64] = 1
    c["BONES_D"] = bd
    invf = (10000.0 ** (-np.arange(0, 32, 2, dtype=np.float64) / 32.0))
    tpos = np.arange(T, dtype=np.float64)
    th = tpos[None, :] * invf[:, None]
    cos16 = np.cos(th).astype(np.float32)
    sin16 = np.sin(th).astype(np.float32)
    cs1 = np.zeros((128, T), dtype=np.float32)
    cs2 = np.zeros((128, T), dtype=np.float32)
    for r in range(128):
        j = r % 16
        cs1[r] = cos16[j]
        cs2[r] = -sin16[j] if (r % 32) < 16 else sin16[j]
    c["CS1"] = cs1.astype(ml_dtypes.bfloat16)
    c["CS2"] = cs2.astype(ml_dtypes.bfloat16)
    RZ = (14.134725142, 21.022039639, 25.01085758, 30.424876126,
          32.935061588, 37.586178159, 40.918719012, 43.327073281,
          48.005150881, 49.773832478, 52.970321478, 56.446247697,
          59.347044003, 60.831778525, 65.112544048, 67.079810529,
          69.546401711, 72.067157674, 75.704690699, 77.144840069,
          79.33737502, 82.910380854, 84.735492981, 87.425274613,
          88.809111208, 92.491899271, 94.651344041, 95.870634228,
          98.831194218, 101.317851006, 103.72553804, 105.446623052)
    gamma = np.asarray(RZ, dtype=np.float32)
    invfd = (1.0 / (gamma / gamma[0])).astype(np.float64)
    d0 = np.zeros((128, T), dtype=np.float32)
    for r in range(128):
        d0[r] = (tpos * invfd[r % 32] / (2 * math.pi)).astype(np.float32)
    c["DCONST0"] = d0
    sgn = np.ones((128, 1), dtype=np.float32)
    for r in range(128):
        if (r % 64) < 32:
            sgn[r] = -1.0
    c["SGN"] = sgn
    s7 = np.zeros((7, 7 * 128), dtype=ml_dtypes.bfloat16)
    for e in range(7):
        s7[e, e * 128:(e + 1) * 128] = 1
    c["SEL7"] = s7
    sel = np.zeros((8, 16, 128), dtype=np.float32)
    for ch in range(8):
        for p in range(128):
            sel[ch, 2 * ch + p // 64, p] = 1
    c["SELS"] = sel
    return c


def _prep_weights(inp):
    f32 = np.float32
    w = {}
    A = lambda x: np.ascontiguousarray(np.asarray(x, dtype=f32))
    w["kv_dT"] = A(np.asarray(inp["L0_ffn_kv_d"], f32).T)
    w["q_dT"] = A(np.asarray(inp["L0_ffn_q_d"], f32).T)
    ku = np.asarray(inp["L0_ffn_k_u"], dtype=f32).reshape(NH, HD, 256)
    qu = np.asarray(inp["L0_ffn_q_u"], dtype=f32).reshape(NH, HD, 256)
    kun = ku[:, 32:, :].reshape(NH * 32, 256).T
    qun = qu[:, 32:, :].reshape(NH * 32, 256).T
    kp = np.zeros((256, 1024), dtype=f32)
    qp = np.zeros((256, 1024), dtype=f32)
    for ch in range(8):
        kp[:, ch * 128 + 32:ch * 128 + 64] = kun[:, (2 * ch) * 32:(2 * ch + 1) * 32]
        kp[:, ch * 128 + 96:ch * 128 + 128] = kun[:, (2 * ch + 1) * 32:(2 * ch + 2) * 32]
        qp[:, ch * 128 + 32:ch * 128 + 64] = qun[:, (2 * ch) * 32:(2 * ch + 1) * 32]
        qp[:, ch * 128 + 96:ch * 128 + 128] = qun[:, (2 * ch + 1) * 32:(2 * ch + 2) * 32]
    w["ku_nP"] = A(kp)
    w["qu_nP"] = A(qp)
    w["v_uT"] = A(np.asarray(inp["L0_ffn_v_u"], f32).T)
    w["rope_qT"] = A(np.asarray(inp["L0_ffn_rope_q_w"], f32).T)
    w["rope_kT"] = A(np.asarray(inp["L0_ffn_rope_k_w"], f32).T)
    w["o_wT"] = A(np.asarray(inp["L0_ffn_o_w"], f32).T)
    w["ln1w"] = A(inp["L0_ffn_ln1_w"]).reshape(1, D)
    w["ln2w"] = A(inp["L0_ffn_ln2_w"]).reshape(1, D)
    wqkv = np.asarray(inp["L0_attn_w_qkv"], dtype=f32).reshape(D, 3, NH, HD)
    perm = np.concatenate([np.arange(0, 64, 2), np.arange(1, 64, 2)])
    q = wqkv[:, 0][:, :, perm]
    k = wqkv[:, 1][:, :, perm]
    v = wqkv[:, 2]
    w["w_qkv_p"] = A(np.concatenate(
        [q.reshape(D, D), k.reshape(D, D), v.reshape(D, D)], axis=1))
    w["w_o"] = A(inp["L0_attn_w_o"])
    w["log_scale_c"] = A(inp["L0_attn_log_scale"]).reshape(NH, 1)
    w["router_wT"] = A(np.asarray(inp["L0_ffn_router_w"], f32).T)
    w["rbias"] = A(inp["L0_ffn_routing_bias"]).reshape(7, 1)
    g = np.asarray(inp["L0_ffn_shared_gate"], dtype=f32)
    u = np.asarray(inp["L0_ffn_shared_up"], dtype=f32)
    d = np.asarray(inp["L0_ffn_shared_down"], dtype=f32)
    rg = np.asarray(inp["L0_ffn_routed_gate"], dtype=f32)
    ru = np.asarray(inp["L0_ffn_routed_up"], dtype=f32)
    rd = np.asarray(inp["L0_ffn_routed_down"], dtype=f32)
    w["gT"] = A(np.concatenate([g.transpose(0, 2, 1),
                                rg.transpose(0, 2, 1)], axis=0))
    w["uT"] = A(np.concatenate([u.transpose(0, 2, 1),
                                ru.transpose(0, 2, 1)], axis=0))
    w["dT"] = A(np.concatenate([d.transpose(0, 2, 1),
                                rd.transpose(0, 2, 1)], axis=0))
    return w


class Sem:
    def __init__(self, h):
        self.h = h
        self.v = 0

    def inc(self, n=1):
        self.v += n
        return self.v


_UID = [0]


def U(name):
    _UID[0] += 1
    return f"{name}_{_UID[0]}"


class Prog:

    def __init__(self, nc, want_taps):
        self.nc = nc
        self.want_taps = set(want_taps)
        self.tap_shapes = {}
        self._tapn = 0
        self.sems = {}
        self.bufs = {}
        self.dram = {}
        self.stop_after = None
        self.stop_hit = False
        self.XBF = None

    def tap(self, name, aps):
        if name == self.stop_after:
            self.stop_hit = True
        if name not in self.want_taps:
            return
        nc = self.nc
        shp = [len(aps)] + list(aps[0].shape)
        t = nc.dram_tensor(f"tap_{name}", shp, F32, kind="ExternalOutput").ap()
        self.tap_shapes[name] = shp
        with nc.semaphore(f"tapsem{self._tapn}") as sem:
            self._tapn += 1
            with nc.Block(no_gpsimd_drain=True) as block:
                @block.gpsimd
                def _(gpsimd):
                    for i, ap in enumerate(aps):
                        gpsimd.dma_start(out=t[i], in_=ap).then_inc(sem, 16)
                    gpsimd.wait_ge(sem, 16 * len(aps))
        emit_barrier(nc)


def emit_barrier(nc):
    with nc.Block(no_gpsimd_drain=True) as block:
        @block.sync
        def _(sync):
            sync.nop()


def emit_ln(pr, src_kind, src, dst, w_row, pre_norm_hook=None):
    nc = pr.nc
    b = pr.bufs
    ld, ev, mm, ac = (pr.sems[s] for s in ("ld", "ev", "mm", "ac"))
    onesc = b["ONEC"]
    sqt = b["SQT"]
    rowa = b["ROWA"]
    rowb = b["ROWB"]

    r_r = b["RR"]
    r_n = b["RN"]
    tmp1 = b["TMP1"]
    wrow = w_row

    with contextlib.ExitStack() as ps_scope:
        ps_sum = ps_scope.enter_context(nc.psum_tensor(U("ln_sum"), [1, T], F32))[:]
        ps_sq = ps_scope.enter_context(nc.psum_tensor(U("ln_sq"), [1, T], F32))[:]

        if src_kind == "hbm":
            ld0 = ld.v
            with nc.Block(no_gpsimd_drain=True) as block:
                @block.gpsimd
                def _(gpsimd):
                    for c in range(NCH):
                        gpsimd.dma_start(
                            out=dst[c],
                            in_=src[c * 128:(c + 1) * 128, :]).then_inc(ld.h, 16)
                    ld.inc(16 * NCH)
            xbf = dst
            chunk_ready = lambda c: ld0 + 16 * NCH
        elif src_kind == "acc":
            ev0 = ev.v
            with nc.Block(no_gpsimd_drain=True) as block:
                @block.vector
                def _(vector):
                    for c in range(NCH):
                        vector.tensor_copy(dst[c], src[c]).then_inc(ev.h, 1)
                    ev.inc(NCH)
            xbf = dst
            chunk_ready = lambda c: None
        else:
            xbf = src
            chunk_ready = lambda c: None

        mm0, ac0, ev0b = mm.v, ac.v, ev.v
        with nc.Block(no_gpsimd_drain=True) as block:
            @block.scalar
            def _(scalar):
                for c in range(NCH):
                    if src_kind == "hbm":
                        scalar.wait_ge(ld.h, chunk_ready(c))
                    elif src_kind == "acc":
                        scalar.wait_ge(ev.h, ev0b - NCH + c + 1)
                    if c >= 1:
                        scalar.wait_ge(mm.h, mm0 + c)
                    scalar.activation(sqt, xbf[c], AF.Square).then_inc(ac.h, 1)

            @block.tensor
            def _(tensor):
                for c in range(NCH):
                    if src_kind == "hbm":
                        tensor.wait_ge(ld.h, chunk_ready(c))
                    elif src_kind == "acc":
                        tensor.wait_ge(ev.h, ev0b - NCH + c + 1)
                    for t in range(NT):
                        tensor.matmul(ps_sum[:, t * TT:(t + 1) * TT], onesc,
                                      xbf[c][:, t * TT:(t + 1) * TT],
                                      start=(c == 0), stop=(c == NCH - 1))
                    tensor.wait_ge(ac.h, ac0 + c + 1)
                    for t in range(NT):
                        mmi = tensor.matmul(
                            ps_sq[:, t * TT:(t + 1) * TT], onesc,
                            sqt[:, t * TT:(t + 1) * TT],
                            start=(c == 0), stop=(c == NCH - 1))
                    mmi.then_inc(mm.h, 1)
        mm.inc(NCH)
        ac.inc(NCH)

        with nc.Block(no_gpsimd_drain=True) as block:
            @block.scalar
            def _(scalar):
                scalar.activation(rowa, ps_sum, AF.Copy, scale=1.0 / D)
                scalar.activation(rowb, ps_sq, AF.Copy, scale=1.0 / D)\
                    .then_inc(ac.h, 1)

            @block.vector
            def _(vector):
                vector.wait_ge(ac.h, ac.inc())
                vector.tensor_tensor(ps_sum, rowa, rowa, ALU.mult)
                vector.tensor_tensor(rowb, rowb, ps_sum, ALU.subtract)\
                    .then_inc(ev.h, 1)

            @block.scalar
            def _(scalar):
                scalar.wait_ge(ev.h, ev.inc())
                scalar.activation(rowb, rowb, AF.Ln,
                                  bias=pr.bufs["CONSTS"][0:1, 0:1])
                scalar.activation(r_r, rowb, AF.Exp, scale=-0.5)\
                    .then_inc(ac.h, 1)

            @block.vector
            def _(vector):
                vector.wait_ge(ac.h, ac.inc())
                vector.tensor_tensor(r_n, rowa, r_r, ALU.mult)
                vector.tensor_scalar(r_n, r_n, -1.0, None, ALU.mult)
    emit_barrier(nc)

    if pre_norm_hook is not None:
        pre_norm_hook()

    with contextlib.ExitStack() as ps_scope:
        ps_cd = [ps_scope.enter_context(
            nc.psum_tensor(U(f"ln_cd{i}"), [128, TT], F32))[:] for i in range(4)]
        mm0, ev0 = mm.v, ev.v
        with nc.Block(no_gpsimd_drain=True) as block:
            @block.tensor
            def _(tensor):
                for c in range(NCH):
                    for t in range(NT):
                        i = c * NT + t
                        if i >= 2:
                            tensor.wait_ge(ev.h, ev0 + i - 1)
                        wr = (wrow[:, c * 128:(c + 1) * 128]
                              if wrow is not None else b["ONE1"][0:1])
                        tensor.matmul(ps_cd[(i % 2) * 2], wr,
                                      r_r[:, t * TT:(t + 1) * TT],
                                      start=True, stop=True)
                        tensor.matmul(ps_cd[(i % 2) * 2 + 1], wr,
                                      r_n[:, t * TT:(t + 1) * TT],
                                      start=True, stop=True)\
                            .then_inc(mm.h, 1)

            @block.vector
            def _(vector):
                for c in range(NCH):
                    for t in range(NT):
                        i = c * NT + t
                        vector.wait_ge(mm.h, mm0 + i + 1)
                        sl = slice(t * TT, (t + 1) * TT)
                        vector.tensor_tensor(tmp1, xbf[c][:, sl],
                                             ps_cd[(i % 2) * 2], ALU.mult)
                        vector.tensor_tensor(dst[c][:, sl], tmp1,
                                             ps_cd[(i % 2) * 2 + 1], ALU.add)\
                            .then_inc(ev.h, 1)
        mm.inc(NCH * NT)
        ev.inc(NCH * NT)
    emit_barrier(nc)


def emit_loads(pr, loads, sem="wld"):
    s = pr.sems[sem]
    nc = pr.nc
    targets = []
    with nc.Block(no_gpsimd_drain=True) as block:
        @block.gpsimd
        def _(gpsimd):
            for dst, src in loads:
                gpsimd.dma_start(out=dst, in_=src).then_inc(s.h, 16)
                targets.append(s.inc(16))
    return targets


def emit_proj(pr, jobs, waits=(), evac=None, psn=2, sync_section=None):
    nc = pr.nc
    mm, ev = pr.sems["mm"], pr.sems["ev"]
    mm0, ev0 = mm.v, ev.v
    n = len(jobs)

    def ps_view(pss_i, out_ap):
        v = pss_i
        if out_ap.shape[1] < TT:
            v = v[:, 0:out_ap.shape[1]]
        if out_ap.shape[0] < 128:
            v = v[0:out_ap.shape[0]]
        return v

    with contextlib.ExitStack() as ps_scope:
        pss = [ps_scope.enter_context(
            nc.psum_tensor(U(f"pj{i}"), [128, TT], F32))[:] for i in range(psn)]
        with nc.Block(no_gpsimd_drain=True) as block:
            if sync_section is not None:
                @block.sync
                def _(sync):
                    sync_section(sync)

            @block.tensor
            def _(tensor):
                for i, (out_ap, pieces) in enumerate(jobs):
                    wl = waits(i) if callable(waits) else (waits if i == 0 else ())
                    for sh, tgt in wl:
                        tensor.wait_ge(sh, tgt)
                    if i >= psn:
                        tensor.wait_ge(ev.h, ev0 + i - psn + 1)
                    np_ = len(pieces)
                    for j, piece in enumerate(pieces):
                        tp = piece[2] if len(piece) > 2 else None
                        pv = pss[i % psn] if evac else ps_view(pss[i % psn], out_ap)
                        mmi = tensor.matmul(pv, piece[0], piece[1],
                                            start=(j == 0), stop=(j == np_ - 1),
                                            tile_position=tp)
                    mmi.then_inc(mm.h, 1)

            @block.vector
            def _(vector):
                for i, (out_ap, pieces) in enumerate(jobs):
                    vector.wait_ge(mm.h, mm0 + i + 1)
                    if evac is None:
                        ins = vector.tensor_copy(out_ap,
                                                 ps_view(pss[i % psn], out_ap))
                    else:
                        ins = evac(vector, i, out_ap, pss[i % psn])
                    ins.then_inc(ev.h, 1)
        mm.inc(n)
        ev.inc(n)
    emit_barrier(nc)

def emit_rotate(pr, chunks, cs1, cs2, swap, tag):
    nc = pr.nc
    b = pr.bufs
    mm, ev = pr.sems["mm"], pr.sems["ev"]
    with contextlib.ExitStack() as ps_scope:
        t2 = ps_scope.enter_context(
            nc.sbuf_tensor(U("T2TMP"), [128, 2 * TT], BF16))[:]
        tb = ps_scope.enter_context(
            nc.sbuf_tensor(U("TMPB"), [128, 2 * TT], BF16))[:]
        pss = [ps_scope.enter_context(
            nc.psum_tensor(U(f"rot{tag}{i}"), [128, TT], F32))[:] for i in range(2)]
        for ci, ch in enumerate(chunks):
            mm0, ev0 = mm.v, ev.v
            with nc.Block(no_gpsimd_drain=True) as block:
                @block.tensor
                def _(tensor):
                    for t in range(NT):
                        if t >= 2:
                            tensor.wait_ge(ev.h, ev0 + t - 1)
                        tensor.matmul(pss[t % 2], swap,
                                      ch[:, t * TT:(t + 1) * TT],
                                      start=True, stop=True).then_inc(mm.h, 1)

                @block.vector
                def _(vector):
                    for t in range(NT):
                        sl = slice(t * TT, (t + 1) * TT)
                        t2s = slice((t % 2) * TT, (t % 2 + 1) * TT)
                        vector.wait_ge(mm.h, mm0 + t + 1)
                        vector.tensor_copy(t2[:, t2s], pss[t % 2])\
                            .then_inc(ev.h, 1)
                        vector.tensor_tensor(tb[:, 0:TT], ch[:, sl],
                                             cs1[:, sl], ALU.mult)
                        vector.tensor_tensor(tb[:, TT:2 * TT], t2[:, t2s],
                                             cs2[:, sl], ALU.mult)
                        vector.tensor_tensor(ch[:, sl], tb[:, 0:TT],
                                             tb[:, TT:2 * TT], ALU.add)
            mm.inc(NT)
            ev.inc(NT)
    emit_barrier(nc)


def emit_bounds(pr, q_chunks, k_chunks, bones, bsel, cpb):
    nc = pr.nc
    b = pr.bufs
    mm, ev, ac = pr.sems["mm"], pr.sems["ev"], pr.sems["ac"]
    sqt = b["SQT"]
    nrm = b["NRM"]
    with contextlib.ExitStack() as ps_scope:
        ps_n = ps_scope.enter_context(nc.psum_tensor(U("bn"), [16, T], F32))[:]
        ps_t = ps_scope.enter_context(nc.psum_tensor(U("bt"), [1, 16], BF16))[:]
        ps_c = ps_scope.enter_context(nc.psum_tensor(U("bc"), [128, 8], F32))[:]
        for side, chunks in ((0, q_chunks), (1, k_chunks)):
            ac0, mm0 = ac.v, mm.v
            with nc.Block(no_gpsimd_drain=True) as block:
                @block.scalar
                def _(scalar):
                    for ci, ch in enumerate(chunks):
                        if ci >= 1:
                            scalar.wait_ge(mm.h, mm0 + ci)
                        scalar.activation(sqt, ch, AF.Square).then_inc(ac.h, 1)

                @block.tensor
                def _(tensor):
                    for ci in range(len(chunks)):
                        tensor.wait_ge(ac.h, ac0 + ci + 1)
                        for t in range(NT):
                            mmi = tensor.matmul(
                                ps_n[:, t * TT:(t + 1) * TT], bsel(ci),
                                sqt[:, t * TT:(t + 1) * TT],
                                start=(ci == 0), stop=(ci == len(chunks) - 1))
                        mmi.then_inc(mm.h, 1)
            ac.inc(len(chunks))
            mm.inc(len(chunks))
            with nc.Block(no_gpsimd_drain=True) as block:
                @block.vector
                def _(vector):
                    vector.reduce_max(nrm[:, side:side + 1], ps_n, axis=AX.X)
        with nc.Block(no_gpsimd_drain=True) as block:
            @block.vector
            def _(vector):
                vector.tensor_tensor(nrm[:, 2:3], nrm[:, 0:1], nrm[:, 1:2],
                                     ALU.mult).then_inc(ev.h, 1)

            @block.scalar
            def _(scalar):
                scalar.wait_ge(ev.h, ev.inc())
                scalar.activation(nrm[:, 2:3], nrm[:, 2:3], AF.Ln)
                scalar.activation(b["C16B"], nrm[0:16, 2:3], AF.Exp,
                                  scale=0.5, bias=b["CONSTS"][0:16, 1:2])\
                    .then_inc(ac.h, 1)

            @block.tensor
            def _(tensor):
                tensor.wait_ge(ac.h, ac.inc())
                tensor.transpose(ps_t, b["C16B"], b["IDN"][0:16, 0:16])\
                    .then_inc(mm.h, 1)

            @block.vector
            def _(vector):
                vector.wait_ge(mm.h, mm.inc())
                vector.reduce_max(b["ROW8F"],
                                  ps_t.rearrange("a (g two) -> a g two", two=2),
                                  axis=AX.X)
                vector.tensor_scalar(b["ROW8B"], b["ROW8F"], -1.0, None,
                                     ALU.mult).then_inc(ev.h, 1)

            @block.tensor
            def _(tensor):
                tensor.wait_ge(ev.h, ev.inc())
                tensor.matmul(ps_c, b["ONE1"][0:1], b["ROW8B"],
                              start=True, stop=True).then_inc(mm.h, 1)

            @block.scalar
            def _(scalar):
                scalar.wait_ge(mm.h, mm.inc())
                scalar.activation(cpb, ps_c, AF.Copy)
    emit_barrier(nc)

def emit_attn_core(pr, pieces_fn, vsb, attscr, cpb):
    nc = pr.nc
    b = pr.bufs
    mm, ev, ac, st = (pr.sems[s] for s in ("mm", "ev", "ac", "st"))
    wld = pr.sems["wld"]
    ytmp = b["TMP1"]
    groups = [(hg, qt) for hg in range(NH // 2) for qt in range(NT)]
    kns = [4 * (qt + 1) for hg, qt in groups]
    mmb, acb, evb, stb = [], [], [], []
    m0, a0, e0, s0 = mm.v, ac.v, ev.v, st.v
    for kn in kns:
        mmb.append(m0); acb.append(a0); evb.append(e0); stb.append(s0)
        m0 += 2 * kn + 1
        a0 += kn + 1
        e0 += 6
        s0 += 32
    with contextlib.ExitStack() as ps_scope:
        masks = ps_scope.enter_context(
            nc.sbuf_tensor(U("MASKS01"), [128, 4 * 2 * TT], BF16))[:]
        pt = ps_scope.enter_context(
            nc.sbuf_tensor(U("PT"), [128, 2 * 1024], BF16))[:]
        srow2 = ps_scope.enter_context(
            nc.sbuf_tensor(U("SROW2"), [1, 2 * TT], F32))[:]
        rbrow = ps_scope.enter_context(
            nc.sbuf_tensor(U("RBROW"), [1, 2 * TT], BF16))[:]
        ato = ps_scope.enter_context(
            nc.sbuf_tensor(U("ATO"), [64, 4 * TT], BF16))[:]
        ps_sc = [ps_scope.enter_context(
            nc.psum_tensor(U(f"at_sc{i}"), [128, 2 * TT], F32))[:]
            for i in range(2)]
        ps_y = [ps_scope.enter_context(
            nc.psum_tensor(U(f"at_y{i}"), [64, TT], F32))[:] for i in range(2)]
        ps_s = [ps_scope.enter_context(
            nc.psum_tensor(U(f"at_s{i}"), [1, TT], F32))[:] for i in range(2)]
        wtm = emit_loads(pr, [
            (masks.rearrange("p (a q) -> p a q", q=2 * TT),
             pr.dram["MASKS01"].rearrange("a p q -> p a q"))])
        with nc.Block(no_gpsimd_drain=True) as block:
            @block.tensor
            def _(tensor):
                tensor.wait_ge(wld.h, wtm[-1])
                for g, (hg, qt) in enumerate(groups):
                    kn = kns[g]
                    for kc in range(kn):
                        if kc == 0:
                            tensor.wait_ge(ev.h, evb[g])
                            tensor.wait_ge(ac.h, acb[g])
                        if kc >= 2:
                            tensor.wait_ge(ac.h, acb[g] + kc - 1)
                        for hh in range(2):
                            h = 2 * hg + hh
                            allp = list(pieces_fn(h, kc, qt))
                            np_ = len(allp)
                            for j, piece in enumerate(allp):
                                tp = piece[2] if len(piece) > 2 else None
                                mmi = tensor.matmul(
                                    ps_sc[kc % 2][:, hh * TT:(hh + 1) * TT],
                                    piece[0], piece[1],
                                    start=(j == 0), stop=(j == np_ - 1),
                                    tile_position=tp)
                        mmi.then_inc(mm.h, 1)
                        if kc >= 4 * qt:
                            tensor.wait_ge(ev.h, evb[g] + (kc - 4 * qt) + 1)
                        else:
                            tensor.wait_ge(ac.h, acb[g] + kc + 1)
                        for hh in range(2):
                            h = 2 * hg + hh
                            tensor.matmul(
                                ps_y[hh],
                                vsb[:, kc * 1024 + 64 * h:
                                    kc * 1024 + 64 * (h + 1)],
                                pt[:, (kc % 2) * 1024 + hh * TT:
                                   (kc % 2) * 1024 + (hh + 1) * TT],
                                start=(kc == 0), stop=False)
                        for hh in range(2):
                            mmi = tensor.matmul(
                                ps_s[hh],
                                b["ONEC"],
                                pt[:, (kc % 2) * 1024 + hh * TT:
                                   (kc % 2) * 1024 + (hh + 1) * TT],
                                start=(kc == 0), stop=False)
                        mmi.then_inc(mm.h, 1)
                    tensor.wait_ge(ac.h, acb[g] + kn + 1)
                    tensor.matmul(ps_sc[0][0:64, 0:TT],
                                  b["ONE1"][0:1, 0:64],
                                  rbrow[:, 0:TT], start=True, stop=True)
                    tensor.matmul(ps_sc[0][0:64, TT:2 * TT],
                                  b["ONE1"][0:1, 0:64],
                                  rbrow[:, TT:2 * TT], start=True,
                                  stop=True).then_inc(mm.h, 1)

            @block.scalar
            def _(scalar):
                for g, (hg, qt) in enumerate(groups):
                    kn = kns[g]
                    for kc in range(kn):
                        scalar.wait_ge(mm.h, mmb[g] + 2 * kc + 1)
                        scalar.activation(
                            pt[:, (kc % 2) * 1024:(kc % 2 + 1) * 1024],
                            ps_sc[kc % 2], AF.Exp,
                            bias=cpb[:, hg:hg + 1], scale=0.125)\
                            .then_inc(ac.h, 1)
                    scalar.wait_ge(ev.h, evb[g] + 5)
                    scalar.activation(srow2, srow2, AF.Ln)
                    scalar.activation(rbrow, srow2, AF.Exp,
                                      scale=-1.0).then_inc(ac.h, 1)

            @block.vector
            def _(vector):
                for g, (hg, qt) in enumerate(groups):
                    kn = kns[g]
                    gi = g
                    for kc in range(4 * qt, kn):
                        o = kc - 4 * qt
                        vector.wait_ge(ac.h, acb[g] + kc + 1)
                        vector.tensor_tensor(
                            pt[:, (kc % 2) * 1024:(kc % 2 + 1) * 1024],
                            pt[:, (kc % 2) * 1024:(kc % 2 + 1) * 1024],
                            masks[:, o * 2 * TT:(o + 1) * 2 * TT],
                            ALU.mult).then_inc(ev.h, 1)
                    vector.wait_ge(mm.h, mmb[g] + 2 * kn)
                    vector.tensor_copy(srow2[:, 0:TT], ps_s[0])
                    vector.tensor_copy(srow2[:, TT:2 * TT], ps_s[1])\
                        .then_inc(ev.h, 1)
                    vector.wait_ge(mm.h, mmb[g] + 2 * kn + 1)
                    if g >= 2:
                        vector.wait_ge(st.h, stb[g] - 32)
                    for hh in range(2):
                        vector.tensor_copy(ytmp[0:64], ps_y[hh])
                        ins = vector.tensor_tensor(
                            ato[:, ((gi % 2) * 2 + hh) * TT:
                                ((gi % 2) * 2 + hh + 1) * TT],
                            ytmp[0:64],
                            ps_sc[0][0:64, hh * TT:(hh + 1) * TT],
                            ALU.mult)
                    ins.then_inc(ev.h, 1)

            @block.sync
            def _(sync):
                for g, (hg, qt) in enumerate(groups):
                    gi = g
                    sync.wait_ge(ev.h, evb[g] + 6)
                    for hh in range(2):
                        sync.dma_start(
                            out=attscr[hg * 128 + 64 * hh:
                                       hg * 128 + 64 * (hh + 1),
                                       qt * TT:(qt + 1) * TT],
                            in_=ato[:, ((gi % 2) * 2 + hh) * TT:
                                    ((gi % 2) * 2 + hh + 1) * TT])\
                            .then_inc(st.h, 16)
                sync.wait_ge(st.h, stb[-1] + 32)
        mm.v = m0
        ac.v = a0
        ev.v = e0
        st.v = s0
    emit_barrier(nc)


def emit_outproj(pr, wdram, attscr, dst, mode, ystream=None):
    nc = pr.nc
    ld, wld, ev, mm, st = (pr.sems[s] for s in ("ld", "wld", "ev", "mm", "st"))
    r3 = lambda a: a.rearrange("(c p) n -> p c n", p=128)
    with contextlib.ExitStack() as sc:
        WOW = sc.enter_context(
            nc.sbuf_tensor(U("WOW"), [128, 8 * 1024], BF16))[:]
        ATB = [sc.enter_context(
            nc.sbuf_tensor(U(f"ATB{j}"), [128, 8 * TT], BF16))[:]
            for j in range(2)]
        if mode == "add_stream":
            YTB = [sc.enter_context(
                nc.sbuf_tensor(U(f"YTB{j}"), [128, 8 * TT], BF16))[:]
                for j in range(2)]
        else:
            STG = [sc.enter_context(
                nc.sbuf_tensor(U(f"STG{j}"), [128, TT], F32))[:]
                for j in range(4)]
        pss = [sc.enter_context(
            nc.psum_tensor(U(f"op{i}"), [128, TT], F32))[:] for i in range(2)]
        wt = emit_loads(pr, [
            (WOW.rearrange("p (c n) -> p c n", n=1024), r3(wdram))])
        ld0, ev0, mm0 = ld.v, ev.v, mm.v
        nper = 32 if mode == "add_stream" else 16

        def sync_sec(sync):
            for t in range(NT):
                sync.dma_start(
                    out=ATB[t % 2].rearrange("p (c n) -> p c n", n=TT),
                    in_=attscr[:, t * TT:(t + 1) * TT].rearrange(
                        "(c p) n -> p c n", p=128)).then_inc(ld.h, 16)
                if mode == "add_stream":
                    sync.dma_start(
                        out=YTB[t % 2].rearrange("p (c n) -> p c n", n=TT),
                        in_=ystream[:, t * TT:(t + 1) * TT].rearrange(
                            "(c p) n -> p c n", p=128)).then_inc(ld.h, 16)
                if t < NT - 2:
                    sync.wait_ge(ev.h, ev0 + (t + 1) * NCH)

        with nc.Block(no_gpsimd_drain=True) as block:
            @block.sync
            def _(sync):
                sync_sec(sync)

            @block.tensor
            def _(tensor):
                tensor.wait_ge(wld.h, wt[-1])
                for t in range(NT):
                    tensor.wait_ge(ld.h, ld0 + nper * (t + 1))
                    for fc in range(NCH):
                        i = t * NCH + fc
                        if i >= 2:
                            tensor.wait_ge(ev.h, ev0 + i - 1)
                        for c in range(NCH):
                            mmi = tensor.matmul(
                                pss[i % 2],
                                WOW[:, c * 1024 + fc * 128:
                                    c * 1024 + (fc + 1) * 128],
                                ATB[t % 2][:, c * TT:(c + 1) * TT],
                                start=(c == 0), stop=(c == NCH - 1))
                        mmi.then_inc(mm.h, 1)

            @block.vector
            def _(vector):
                st0v = st.v
                for t in range(NT):
                    for fc in range(NCH):
                        i = t * NCH + fc
                        vector.wait_ge(mm.h, mm0 + i + 1)
                        sl = slice(t * TT, (t + 1) * TT)
                        if mode == "add_stream":
                            vector.tensor_tensor(
                                dst[fc][:, sl],
                                YTB[t % 2][:, fc * TT:(fc + 1) * TT],
                                pss[i % 2], ALU.add).then_inc(ev.h, 1)
                        else:
                            if i >= 4:
                                vector.wait_ge(st.h, st0v + 16 * (i - 3))
                            vector.tensor_copy(STG[i % 4],
                                               pss[i % 2]).then_inc(ev.h, 1)

            if mode == "rmw":
                st0r = st.v
                @block.gpsimd
                def _(gpsimd):
                    for t in range(NT):
                        for fc in range(NCH):
                            i = t * NCH + fc
                            gpsimd.wait_ge(ev.h, ev0 + i + 1)
                            gpsimd.dma_start(
                                out=dst[fc * 128:(fc + 1) * 128,
                                        t * TT:(t + 1) * TT],
                                in_=STG[i % 4], accum_op=ALU.add)\
                                .then_inc(st.h, 16)
                    st.inc(16 * NT * NCH)
                    gpsimd.wait_ge(st.h, st0r + 16 * NT * NCH)
        ld.inc(nper * NT)
        mm.inc(NT * NCH)
        ev.inc(NT * NCH)
    emit_barrier(nc)


def emit_resid_add(pr, acc, xdram):
    nc = pr.nc
    ld, ev = pr.sems["ld"], pr.sems["ev"]
    with contextlib.ExitStack() as sc:
        XT2 = sc.enter_context(
            nc.sbuf_tensor(U("XT2"), [128, 2 * T], F32, side="right"))[:]
        ld0, ev0 = ld.v, ev.v
        with nc.Block(no_gpsimd_drain=True) as block:
            @block.sync
            def _(sync):
                for c in range(NCH):
                    sync.dma_start(out=XT2[:, (c % 2) * T:(c % 2 + 1) * T],
                                   in_=xdram[c * 128:(c + 1) * 128, :])\
                        .then_inc(ld.h, 16)
                    if c < NCH - 2:
                        sync.wait_ge(ev.h, ev0 + c + 1)

            @block.vector
            def _(vector):
                for c in range(NCH):
                    vector.wait_ge(ld.h, ld0 + 16 * (c + 1))
                    vector.tensor_tensor(
                        acc[c], acc[c], XT2[:, (c % 2) * T:(c % 2 + 1) * T],
                        ALU.add).then_inc(ev.h, 1)
        ld.inc(16 * NCH)
        ev.inc(NCH)
    emit_barrier(nc)


def emit_ffn(pr, tag, xsrc, xdst):
    nc = pr.nc
    b = pr.bufs
    dram = pr.dram
    XBF = pr.XBF
    ld, wld, ev, mm, ac, st = (pr.sems[s] for s in
                               ("ld", "wld", "ev", "mm", "ac", "st"))
    r3 = lambda a, p=128: a.rearrange("(c p) n -> p c n", p=p)

    with nc.named_scope(f"ln0{tag}"):
        emit_ln(pr, "hbm", xsrc, XBF, None)
    pr.tap(f"y{tag}", XBF)
    if pr.stop_hit:
        return

    s_mla = contextlib.ExitStack()
    s_rn = contextlib.ExitStack()
    L = lambda st_, name, shape, dt: st_.enter_context(
        nc.sbuf_tensor(U(name), list(shape), dt))[:]
    QATT = [L(s_mla, f"QATT{i}", (128, T), BF16) for i in range(NCH)]
    VSB = L(s_mla, "VSB", (128, 16 * 1024), BF16)
    PAB = L(s_mla, "PAB", (128, 4 * 128), BF16)
    ROTQ = [L(s_rn, f"ROTQ{i}", (128, T), BF16) for i in range(4)]
    ROTK = [L(s_rn, f"ROTK{i}", (128, T), BF16) for i in range(4)]
    WKV = VSB[:, 0:2048]
    WQD = VSB[:, 2048:4096]
    WRK = VSB[:, 4096:8192]
    s_lat = contextlib.ExitStack()
    s_w2 = contextlib.ExitStack()
    R = lambda st_, name, shape, dt: st_.enter_context(
        nc.sbuf_tensor(U(name), list(shape), dt, side="right"))[:]
    QL = R(s_lat, "QL", (128, 2 * T), BF16)
    KVL = R(s_lat, "KVL", (128, 2 * T), BF16)
    WNKP = R(s_w2, "WNKP", (128, 2 * 1024), BF16)
    WNQP = R(s_w2, "WNQP", (128, 2 * 1024), BF16)
    CS1 = R(s_w2, "CS1", (128, T), BF16)
    CS2 = R(s_w2, "CS2", (128, T), BF16)
    WRQ = R(s_w2, "WRQ", (128, 2 * 512), BF16)
    wt1 = emit_loads(pr, [
        (WKV.rearrange("p (c n) -> p c n", n=256), r3(dram["kv_dT"])),
        (WQD.rearrange("p (c n) -> p c n", n=256), r3(dram["q_dT"])),
        (WRK.rearrange("p (c n) -> p c n", n=512), r3(dram["rope_kT"])),
        (WNKP.rearrange("p (c n) -> p c n", n=1024), r3(dram["ku_nP"])),
        (WNQP.rearrange("p (c n) -> p c n", n=1024), r3(dram["qu_nP"])),
        (WRQ.rearrange("p (c n) -> p c n", n=512), r3(dram["rope_qT"])),
        (CS1, dram["CS1"]), (CS2, dram["CS2"]),
        (PAB.rearrange("p (a n) -> p a n", n=128),
         dram["PAB"].rearrange("a p n -> p a n")),
    ])

    A1 = QATT
    with nc.named_scope(f"ln1{tag}"):
        emit_ln(pr, "sbuf_bf", XBF, A1, b["LN1W"])
    pr.tap(f"a1{tag}", A1)
    if pr.stop_hit:
        return

    jobs = []
    for lc in range(2):
        for t in range(NT):
            sl = slice(t * TT, (t + 1) * TT)
            jobs.append((KVL[:, lc * T:][:, sl],
                         [(WKV[:, c * 256 + lc * 128:c * 256 + (lc + 1) * 128],
                           A1[c][:, sl]) for c in range(NCH)]))
            jobs.append((QL[:, lc * T:][:, sl],
                         [(WQD[:, c * 256 + lc * 128:c * 256 + (lc + 1) * 128],
                           A1[c][:, sl]) for c in range(NCH)]))
    for rc in range(4):
        for t in range(NT):
            sl = slice(t * TT, (t + 1) * TT)
            jobs.append((ROTK[rc][:, sl],
                         [(WRK[:, c * 512 + rc * 128:c * 512 + (rc + 1) * 128],
                           A1[c][:, sl]) for c in range(NCH)]))
    with nc.named_scope(f"proj1{tag}"):
        emit_proj(pr, jobs, waits=[(wld.h, wt1[-1])])
    pr.tap(f"kvl{tag}", [KVL, QL])
    if pr.stop_hit:
        return

    jobs = []
    for rc in range(4):
        for t in range(NT):
            sl = slice(t * TT, (t + 1) * TT)
            jobs.append((ROTQ[rc][:, sl],
                         [(WRQ[:, lc * 512 + rc * 128:lc * 512 + (rc + 1) * 128],
                           QL[:, lc * T:][:, sl]) for lc in range(2)]))
    with nc.named_scope(f"proj2{tag}"):
        emit_proj(pr, jobs)
    pr.tap(f"rotkpre{tag}", ROTK)
    if pr.stop_hit:
        return

    with nc.named_scope(f"rot{tag}"):
        emit_rotate(pr, ROTK + ROTQ, CS1, CS2, b["SWAP32"], "m")
    pr.tap(f"rotk{tag}", ROTK)
    pr.tap(f"rotq{tag}", ROTQ)
    if pr.stop_hit:
        return

    with nc.Block(no_gpsimd_drain=True) as block:
        @block.sync
        def _(sync):
            for c in range(NCH):
                sync.dma_start(out=pr.dram_yscr[c * 128:(c + 1) * 128, :],
                               in_=XBF[c]).then_inc(st.h, 16)
            st.inc(16 * NCH)
            sync.wait_ge(st.h, st.v)
    emit_barrier(nc)
    KATT = XBF
    jobs = []
    for c in range(NCH):
        var = c % 2
        for t in range(NT):
            sl = slice(t * TT, (t + 1) * TT)
            jobs.append((KATT[c][:, sl],
                         [(PAB[:, var * 128:(var + 1) * 128],
                           ROTK[c // 2][:, sl])] +
                         [(WNKP[:, lc * 1024 + c * 128:lc * 1024 + (c + 1) * 128],
                           KVL[:, lc * T:][:, sl]) for lc in range(2)]))
    with nc.named_scope(f"asmk{tag}"):
        emit_proj(pr, jobs)
    jobs = []
    for c in range(NCH):
        var = c % 2
        for t in range(NT):
            sl = slice(t * TT, (t + 1) * TT)
            jobs.append((QATT[c][:, sl],
                         [(PAB[:, var * 128:(var + 1) * 128],
                           ROTQ[c // 2][:, sl])] +
                         [(WNQP[:, lc * 1024 + c * 128:lc * 1024 + (c + 1) * 128],
                           QL[:, lc * T:][:, sl]) for lc in range(2)]))
    with nc.named_scope(f"asmq{tag}"):
        emit_proj(pr, jobs)
    s_rn.close()
    pr.tap(f"katt{tag}", KATT)
    pr.tap(f"qatt{tag}", QATT)
    if pr.stop_hit:
        return

    s_vw = contextlib.ExitStack()
    WVU = L(s_vw, "WVU", (128, 2 * 1024), BF16)
    wt_v = emit_loads(pr, [
        (WVU.rearrange("p (c n) -> p c n", n=1024), r3(dram["v_uT"]))])
    jobs = []
    for tc in range(16):
        for vh in range(2):
            jobs.append((VSB[:, tc * 1024 + vh * TT:tc * 1024 + (vh + 1) * TT],
                         [(KVL[:, lc * T + tc * 128:lc * T + (tc + 1) * 128],
                           WVU[:, lc * 1024 + vh * TT:lc * 1024 + (vh + 1) * TT])
                          for lc in range(2)]))
    with nc.named_scope(f"vproj{tag}"):
        emit_proj(pr, jobs, waits=[(wld.h, wt_v[-1])])
    s_vw.close()
    pr.tap(f"vsb{tag}", [VSB[:, i * 1024:(i + 1) * 1024] for i in range(16)])
    if pr.stop_hit:
        return
    s_w2.close()
    s_lat.close()

    CPB = b["CPB"]
    with nc.named_scope(f"bounds{tag}"):
        emit_bounds(pr, QATT, KATT, b["BONES_D"],
                    lambda ci: b["BONES_D"][:, ci * 16:(ci + 1) * 16], CPB)

    def mla_pieces(h, kc, qt):
        cg, rb = h // 2, 64 * (h % 2)
        ksl = slice(kc * 128, (kc + 1) * 128)
        qsl = slice(qt * TT, (qt + 1) * TT)
        return [(KATT[cg][rb:rb + 64, ksl], QATT[cg][rb:rb + 64, qsl],
                 (rb, 0))]

    with nc.named_scope(f"core{tag}"):
        emit_attn_core(pr, mla_pieces, VSB, pr.dram_attscr, CPB)
    pr.tap(f"atty{tag}",
           [pr.dram_attscr[c * 128:(c + 1) * 128, :] for c in range(NCH)])
    s_mla.close()
    if pr.stop_hit:
        return

    with nc.named_scope(f"oproj{tag}"):
        emit_outproj(pr, dram["o_wT"], pr.dram_attscr, XBF, "add_stream",
                     ystream=pr.dram_yscr)
    pr.tap(f"u{tag}", XBF)
    if pr.stop_hit:
        return

    s_xt = contextlib.ExitStack()
    XT2 = s_xt.enter_context(
        nc.sbuf_tensor(U("XT2"), [128, 2 * T], F32, side="right"))[:]
    STG2 = s_xt.enter_context(
        nc.sbuf_tensor(U("STG2"), [128, 2 * T], F32, side="right"))[:]

    def acc_init():
        ld0, st0, ev0 = ld.v, st.v, ev.v
        with nc.Block(no_gpsimd_drain=True) as block:
            @block.sync
            def _(sync):
                for c in range(NCH):
                    sync.dma_start(out=XT2[:, (c % 2) * T:(c % 2 + 1) * T],
                                   in_=xsrc[c * 128:(c + 1) * 128, :])\
                        .then_inc(ld.h, 16)
                    if c < NCH - 2:
                        sync.wait_ge(ev.h, ev0 + c + 1)
                ld.inc(16 * NCH)

            @block.vector
            def _(vector):
                for c in range(NCH):
                    vector.wait_ge(ld.h, ld0 + 16 * (c + 1))
                    if c >= 2:
                        vector.wait_ge(st.h, st0 + 16 * (c - 1))
                    vector.scalar_tensor_tensor(
                        STG2[:, (c % 2) * T:(c % 2 + 1) * T], XBF[c], 0.5,
                        XT2[:, (c % 2) * T:(c % 2 + 1) * T],
                        ALU.mult, ALU.add).then_inc(ev.h, 1)
                ev.inc(NCH)

            @block.gpsimd
            def _(gpsimd):
                for c in range(NCH):
                    gpsimd.wait_ge(ev.h, ev0 + c + 1)
                    gpsimd.dma_start(out=xdst[c * 128:(c + 1) * 128, :],
                                     in_=STG2[:, (c % 2) * T:(c % 2 + 1) * T])\
                        .then_inc(st.h, 16)
                st.inc(16 * NCH)
                gpsimd.wait_ge(st.h, st.v)
        emit_barrier(nc)

    with nc.named_scope(f"ln2{tag}"):
        emit_ln(pr, "sbuf_bf", XBF, XBF, b["LN2W"], pre_norm_hook=acc_init)
    s_xt.close()
    A2 = XBF
    pr.tap(f"a2{tag}", A2)
    pr.tap(f"accmid{tag}",
           [xdst[c * 128:(c + 1) * 128, :] for c in range(NCH)])
    if pr.stop_hit:
        return

    s_wr = contextlib.ExitStack()
    b["WROW"] = s_wr.enter_context(
        nc.sbuf_tensor(U("WROW"), [7, T], BF16, side="right"))[:]
    with nc.named_scope(f"gate{tag}"):
        emit_gating(pr, tag, A2)
    if pr.stop_hit:
        return
    with nc.named_scope(f"moe{tag}"):
        emit_moe(pr, tag, A2, xdst)
    s_wr.close()
    pr.tap(f"acc{tag}",
           [xdst[c * 128:(c + 1) * 128, :] for c in range(NCH)])


def emit_gating(pr, tag, a2):
    nc = pr.nc
    b = pr.bufs
    dram = pr.dram
    ld = pr.sems["ld"]
    mm, ev, ac, wld = (pr.sems[s] for s in ("mm", "ev", "ac", "wld"))
    wrow = b["WROW"]
    s_g = contextlib.ExitStack()
    grows_p = s_g.enter_context(nc.sbuf_tensor(U("GPROB"), [7, T], F32))[:]
    grows_s = s_g.enter_context(nc.sbuf_tensor(U("GSCR"), [7, T], F32))[:]
    gsc = b["GSC"]
    gtmp = b["GTMP"]
    WRT = b["WRT"]
    RB7 = b["RB7"]
    wt = emit_loads(pr, [
        (WRT.rearrange("p (c n) -> p c n", n=7),
         dram["router_wT"].rearrange("(c p) n -> p c n", p=128)),
    ])
    ld0 = ld.v
    with nc.Block(no_gpsimd_drain=True) as block:
        @block.sync
        def _(sync):
            sync.dma_start(out=RB7, in_=dram["rbias"]).then_inc(ld.h, 16)
        ld.inc(16)

    with contextlib.ExitStack() as ps_scope:
        ps_l = ps_scope.enter_context(nc.psum_tensor(U("gl"), [7, T], F32))[:]
        mm0 = mm.v
        with nc.Block(no_gpsimd_drain=True) as block:
            @block.tensor
            def _(tensor):
                tensor.wait_ge(wld.h, wt[-1])
                for t in range(NT):
                    for c in range(NCH):
                        mmi = tensor.matmul(
                            ps_l[:, t * TT:(t + 1) * TT],
                            WRT[:, c * 7:(c + 1) * 7],
                            a2[c][:, t * TT:(t + 1) * TT],
                            start=(c == 0), stop=(c == NCH - 1))
                mmi.then_inc(mm.h, 1)

            @block.vector
            def _(vector):
                vector.wait_ge(ld.h, ld0 + 16)
                vector.tensor_scalar(RB7, RB7, -1.0, None, ALU.mult)\
                    .then_inc(ev.h, 1)

            @block.scalar
            def _(scalar):
                scalar.wait_ge(mm.h, mm.inc())
                scalar.wait_ge(ev.h, ev.inc())
                scalar.activation(grows_p, ps_l, AF.Exp, scale=-1.0,
                                  bias=RB7).then_inc(ac.h, 1)

            @block.vector
            def _(vector):
                vector.wait_ge(ac.h, ac.inc())
                vector.tensor_scalar(grows_s, grows_p, 1.0,
                                     None, ALU.add).then_inc(ev.h, 1)

            @block.scalar
            def _(scalar):
                scalar.wait_ge(ev.h, ev.inc())
                scalar.activation(grows_s, grows_s, AF.Ln)
                scalar.activation(grows_p, grows_s, AF.Exp,
                                  scale=-1.0).then_inc(ac.h, 1)
        ac.inc(1)
    emit_barrier(nc)
    pr.tap(f"probs{tag}", [grows_p])

    prtf = b["PRTF"]
    wgtb = b["WGTB"]
    with contextlib.ExitStack() as ps_scope:
        ps_t = [ps_scope.enter_context(
            nc.psum_tensor(U(f"gt{i}"), [128, 8], F32))[:] for i in range(2)]
        mm0, ev0 = mm.v, ev.v
        with nc.Block(no_gpsimd_drain=True) as block:
            @block.tensor
            def _(tensor):
                for tc in range(16):
                    if tc >= 2:
                        tensor.wait_ge(ev.h, ev0 + tc - 1)
                    tensor.transpose(ps_t[tc % 2][:, 0:7],
                                     grows_p[:, tc * 128:(tc + 1) * 128],
                                     b["IDNF"][0:7, 0:7]).then_inc(mm.h, 1)

            @block.vector
            def _(vector):
                for tc in range(16):
                    vector.wait_ge(mm.h, mm0 + tc + 1)
                    vector.tensor_copy(prtf[:, tc * 8:tc * 8 + 7],
                                       ps_t[tc % 2][:, 0:7]).then_inc(ev.h, 1)
        mm.inc(16)
        ev.inc(16)
    emit_barrier(nc)
    pr.tap(f"prtf{tag}", [prtf])

    gmsk = b["GMSK"]
    bneg = b["BNEG"]
    bzero = b["BZERO"]
    ev0, ac0 = ev.v, ac.v
    with nc.Block(no_gpsimd_drain=True) as block:
        @block.vector
        def _(vector):
            gg = pr.sems["gg"]
            for tc in range(16):
                p7 = prtf[:, tc * 8:tc * 8 + 7]
                vector.reduce_max(gsc[:, 0:1], p7, axis=AX.X)\
                    .then_inc(gg.h, 1)
                vector.wait_ge(gg.h, gg.inc())
                vector.tensor_scalar(gmsk[:, 0:7], p7, gsc[:, 0:1], None,
                                     ALU.is_equal)
                vector.select(gtmp[:, 0:7], gmsk[:, 0:7], bneg[:, 0:7], p7)\
                    .then_inc(gg.h, 1)
                vector.wait_ge(gg.h, gg.inc())
                vector.reduce_max(gsc[:, 1:2], gtmp[:, 0:7], axis=AX.X)\
                    .then_inc(gg.h, 1)
                vector.wait_ge(gg.h, gg.inc())
                vector.tensor_scalar(gmsk[:, 0:7], p7, gsc[:, 1:2], None,
                                     ALU.is_ge)
                vector.select(gtmp[:, 8:15], gmsk[:, 0:7], p7, bzero[:, 0:7])\
                    .then_inc(gg.h, 1)
                vector.wait_ge(gg.h, gg.inc())
                vector.reduce_sum(gsc[:, 2:3], gtmp[:, 8:15], axis=AX.X)\
                    .then_inc(ev.h, 1)
                vector.wait_ge(ac.h, ac0 + tc + 1)
                vector.tensor_scalar(wgtb[:, tc * 8:tc * 8 + 7],
                                     gtmp[:, 8:15], gsc[:, 3:4], 0.5,
                                     ALU.mult, ALU.mult).then_inc(ev.h, 1)

        @block.scalar
        def _(scalar):
            for tc in range(16):
                scalar.wait_ge(ev.h, ev0 + 2 * tc + 1)
                scalar.activation(gsc[:, 3:4], gsc[:, 2:3], AF.Ln)
                scalar.activation(gsc[:, 3:4], gsc[:, 3:4], AF.Exp,
                                  scale=-1.0).then_inc(ac.h, 1)
    ev.inc(32)
    ac.inc(16)
    emit_barrier(nc)
    pr.tap(f"wgtb{tag}", [wgtb])

    with contextlib.ExitStack() as ps_scope:
        ps_w = [ps_scope.enter_context(
            nc.psum_tensor(U(f"gw{i}"), [16, 128], BF16))[:] for i in range(2)]
        mm0, ev0 = mm.v, ev.v
        with nc.Block(no_gpsimd_drain=True) as block:
            @block.tensor
            def _(tensor):
                for tc in range(16):
                    if tc >= 2:
                        tensor.wait_ge(ev.h, ev0 + tc - 1)
                    tensor.transpose(ps_w[tc % 2][0:8],
                                     wgtb[:, tc * 8:(tc + 1) * 8],
                                     b["IDN"]).then_inc(mm.h, 1)

            @block.vector
            def _(vector):
                for tc in range(16):
                    vector.wait_ge(mm.h, mm0 + tc + 1)
                    vector.tensor_copy(wrow[:, tc * 128:(tc + 1) * 128],
                                       ps_w[tc % 2][0:7]).then_inc(ev.h, 1)
        mm.inc(16)
        ev.inc(16)
    s_g.close()
    emit_barrier(nc)
    pr.tap(f"wrow{tag}", [wrow])

def emit_moe(pr, tag, a2, xdst):
    nc = pr.nc
    b = pr.bufs
    dram = pr.dram
    mm, ev, ac, wld, st = (pr.sems[s] for s in ("mm", "ev", "ac", "wld", "st"))
    wrow = b["WROW"]
    NIB = 4
    with contextlib.ExitStack() as s_moe:
        HB = [s_moe.enter_context(
            nc.sbuf_tensor(U(f"HB{i}"), [128, T], BF16))[:] for i in range(8)]
        WGB = [s_moe.enter_context(
            nc.sbuf_tensor(U(f"WGB{j}"), [128, 8 * 256], BF16))[:]
            for j in range(2)]
        WUB = [s_moe.enter_context(
            nc.sbuf_tensor(U(f"WUB{j}"), [128, 8 * 256], BF16))[:]
            for j in range(2)]
        WDB = [s_moe.enter_context(
            nc.sbuf_tensor(U(f"WDB{j}"), [128, 8 * 128], BF16))[:]
            for j in range(2)]
        WSB = s_moe.enter_context(nc.sbuf_tensor(U("WSB"), [128, T], BF16))[:]
        SGT = s_moe.enter_context(nc.sbuf_tensor(U("SGT"), [128, 2 * TT], BF16))[:]
        UST = s_moe.enter_context(nc.sbuf_tensor(U("UST"), [128, 2 * TT], BF16))[:]
        STG = [s_moe.enter_context(
            nc.sbuf_tensor(U(f"STG{j}"), [128, T], F32))[:] for j in range(2)]
        ps_g = [s_moe.enter_context(
            nc.psum_tensor(U(f"mg{i}"), [128, TT], F32))[:] for i in range(2)]
        ps_u = [s_moe.enter_context(
            nc.psum_tensor(U(f"mu{i}"), [128, TT], F32))[:] for i in range(2)]
        ps_d = s_moe.enter_context(
            nc.psum_tensor(U("md"), [128, 4 * TT], F32))
        ps_w = ps_g[0]

        r3 = lambda a: a.rearrange("(c p) n -> p c n", p=128)

        gu_tgt = {}
        d_tgt = {}
        wt = emit_loads(pr, [
            (WGB[0].rearrange("p (c n) -> p c n", n=256),
             r3(dram["gT"][0][:, 0:256])),
            (WUB[0].rearrange("p (c n) -> p c n", n=256),
             r3(dram["uT"][0][:, 0:256])),
        ])
        gu_tgt[0] = wt[-1]
        st_moe0 = st.v

        for e in range(8):
            mm0, ac0 = mm.v, ac.v
            if e > 0:
                with nc.Block(no_gpsimd_drain=True) as block:
                    @block.tensor
                    def _(tensor):
                        for t in range(NT):
                            if t >= 1:
                                tensor.wait_ge(ac.h, ac0 + t)
                            tensor.matmul(ps_w,
                                          b["SEL7"][:, (e - 1) * 128:e * 128],
                                          wrow[0:7, t * TT:(t + 1) * TT],
                                          start=True, stop=True)\
                                .then_inc(mm.h, 1)

                    @block.scalar
                    def _(scalar):
                        for t in range(NT):
                            scalar.wait_ge(mm.h, mm0 + t + 1)
                            scalar.activation(WSB[:, t * TT:(t + 1) * TT],
                                              ps_w, AF.Copy).then_inc(ac.h, 1)
                mm.inc(NT)
                ac.inc(NT)

            for half in range(2):
                hb = 2 * e + half
                for ib in range(NIB):
                    glb = 8 * e + 4 * half + ib
                    nxt = glb + 1 if glb < 63 else None
                    loads = []
                    if nxt is not None:
                        ee, bb = nxt // 8, nxt % 8
                        loads += [
                            (WGB[(ib + 1) % 2].rearrange(
                                "p (c n) -> p c n", n=256),
                             r3(dram["gT"][ee][:, bb * 256:(bb + 1) * 256])),
                            (WUB[(ib + 1) % 2].rearrange(
                                "p (c n) -> p c n", n=256),
                             r3(dram["uT"][ee][:, bb * 256:(bb + 1) * 256])),
                        ]
                    if ib == 1:
                        loads.append(
                            (WDB[0].rearrange("p (c n) -> p c n", n=128),
                             dram["dT"][e][half * 1024:(half + 1) * 1024,
                                           0:128].rearrange(
                                 "(c p) n -> p c n", p=128)))
                    mm0, ac0, ev0 = mm.v, ac.v, ev.v
                    wld0 = wld.v
                    with nc.Block(no_gpsimd_drain=True) as block:
                        @block.gpsimd
                        def _(gpsimd):
                            if loads:
                                for dst, src in loads:
                                    gpsimd.dma_start(out=dst, in_=src)\
                                        .then_inc(wld.h, 16)
                                wld.inc(16 * len(loads))
                                if nxt is not None:
                                    gu_tgt[nxt] = wld0 + 32
                                if ib == 1:
                                    d_tgt[(hb, 0)] = wld.v

                        @block.tensor
                        def _(tensor):
                            tensor.wait_ge(wld.h, gu_tgt[glb])
                            for ic2 in range(2):
                                for t in range(NT):
                                    i = ic2 * NT + t
                                    sl = slice(t * TT, (t + 1) * TT)
                                    if i >= 2:
                                        tensor.wait_ge(ac.h, ac0 + i - 1)
                                        tensor.wait_ge(ev.h, ev0 + 2 * i - 3)
                                    for c in range(NCH):
                                        mmi = tensor.matmul(
                                            ps_g[i % 2],
                                            WGB[ib % 2][:, c * 256 + ic2 * 128:
                                                        c * 256 + (ic2 + 1) * 128],
                                            a2[c][:, sl],
                                            start=(c == 0), stop=(c == NCH - 1))
                                    mmi.then_inc(mm.h, 1)
                                    for c in range(NCH):
                                        mmi = tensor.matmul(
                                            ps_u[i % 2],
                                            WUB[ib % 2][:, c * 256 + ic2 * 128:
                                                        c * 256 + (ic2 + 1) * 128],
                                            a2[c][:, sl],
                                            start=(c == 0), stop=(c == NCH - 1))
                                    mmi.then_inc(mm.h, 1)

                        @block.scalar
                        def _(scalar):
                            for i in range(8):
                                scalar.wait_ge(mm.h, mm0 + 2 * i + 1)
                                if i >= 2:
                                    scalar.wait_ge(ev.h, ev0 + 2 * i - 2)
                                scalar.activation(
                                    SGT[:, (i % 2) * TT:(i % 2 + 1) * TT],
                                    ps_g[i % 2], AF.Silu).then_inc(ac.h, 1)

                        @block.vector
                        def _(vector):
                            for ic2 in range(2):
                                for t in range(NT):
                                    i = ic2 * NT + t
                                    ic = ib * 2 + ic2
                                    sl = slice(t * TT, (t + 1) * TT)
                                    vector.wait_ge(mm.h, mm0 + 2 * i + 2)
                                    if e == 0:
                                        vector.tensor_scalar(
                                            UST[:, (i % 2) * TT:(i % 2 + 1) * TT],
                                            ps_u[i % 2], 0.5, None, ALU.mult)\
                                            .then_inc(ev.h, 1)
                                    else:
                                        vector.tensor_tensor(
                                            UST[:, (i % 2) * TT:(i % 2 + 1) * TT],
                                            ps_u[i % 2], WSB[:, sl], ALU.mult)\
                                            .then_inc(ev.h, 1)
                                    vector.wait_ge(ac.h, ac0 + i + 1)
                                    vector.tensor_tensor(
                                        HB[ic][:, sl],
                                        SGT[:, (i % 2) * TT:(i % 2 + 1) * TT],
                                        UST[:, (i % 2) * TT:(i % 2 + 1) * TT],
                                        ALU.mult).then_inc(ev.h, 1)
                    mm.inc(16)
                    ac.inc(8)
                    ev.inc(16)

                for fc in range(NCH):
                    gidx = hb * NCH + fc
                    loads = []
                    if fc < NCH - 1:
                        loads.append(
                            (WDB[(fc + 1) % 2].rearrange(
                                "p (c n) -> p c n", n=128),
                             dram["dT"][e][half * 1024:(half + 1) * 1024,
                                           (fc + 1) * 128:(fc + 2) * 128]
                             .rearrange("(c p) n -> p c n", p=128)))
                    mm0, ev0 = mm.v, ev.v
                    with nc.Block(no_gpsimd_drain=True) as block:
                        @block.tensor
                        def _(tensor):
                            tensor.wait_ge(wld.h, d_tgt[(hb, fc)])
                            for th in range(2):
                                if fc > 0 or th > 0:
                                    tensor.wait_ge(ev.h, ev0 - 1 + th)
                                for ic in range(8):
                                    for t2 in range(2):
                                        t = th * 2 + t2
                                        mmi = tensor.matmul(
                                            ps_d[:, t * TT:(t + 1) * TT],
                                            WDB[fc % 2][:, ic * 128:
                                                        (ic + 1) * 128],
                                            HB[ic][:, t * TT:(t + 1) * TT],
                                            start=(ic == 0), stop=(ic == 7))
                                mmi.then_inc(mm.h, 1)

                        @block.vector
                        def _(vector):
                            for th in range(2):
                                vector.wait_ge(mm.h, mm0 + th + 1)
                                if gidx >= 2 and th == 0:
                                    vector.wait_ge(st.h,
                                                   st_moe0 + 16 * (gidx - 1))
                                vector.tensor_copy(
                                    STG[gidx % 2][:, th * 2 * TT:
                                                  (th + 1) * 2 * TT],
                                    ps_d[:, th * 2 * TT:(th + 1) * 2 * TT])\
                                    .then_inc(ev.h, 1)

                        @block.gpsimd
                        def _(gpsimd):
                            if loads:
                                for dst, src in loads:
                                    gpsimd.dma_start(out=dst, in_=src)\
                                        .then_inc(wld.h, 16)
                                wld.inc(16)
                                d_tgt[(hb, fc + 1)] = wld.v
                            gpsimd.wait_ge(ev.h, ev0 + 2)
                            gpsimd.dma_start(
                                out=xdst[fc * 128:(fc + 1) * 128, :],
                                in_=STG[gidx % 2], accum_op=ALU.add)\
                                .then_inc(st.h, 16)
                            st.inc(16)
                    mm.inc(2)
                    ev.inc(2)
        with nc.Block(no_gpsimd_drain=True) as block:
            @block.sync
            def _(sync):
                sync.wait_ge(st.h, st.v)
    emit_barrier(nc)


def emit_dense(pr, xscr):
    nc = pr.nc
    b = pr.bufs
    dram = pr.dram
    XBF = pr.XBF
    ld, wld, ev, mm, ac = (pr.sems[s] for s in ("ld", "wld", "ev", "mm", "ac"))
    r3 = lambda a: a.rearrange("(c p) n -> p c n", p=128)
    R = lambda st_, name, shape, dt: st_.enter_context(
        nc.sbuf_tensor(U(name), list(shape), dt, side="right"))[:]
    s_att = contextlib.ExitStack()
    L = lambda name, shape, dt: s_att.enter_context(
        nc.sbuf_tensor(U(name), list(shape), dt))[:]
    QPRE = [L(f"QPREd{i}", (128, T), BF16) for i in range(NCH)]
    KPRE = [L(f"KPREd{i}", (128, T), BF16) for i in range(NCH)]

    s_qk = contextlib.ExitStack()
    WQK = R(s_qk, "WQKd", (128, 8 * 2048), BF16)
    wt_qk = emit_loads(pr, [
        (WQK.rearrange("p (c n) -> p c n", n=2048),
         r3(dram["w_qkv_p"][:, 0:2048]))])
    jobs = []
    for oc in range(NCH):
        for t in range(NT):
            sl = slice(t * TT, (t + 1) * TT)
            jobs.append((QPRE[oc][:, sl],
                         [(WQK[:, c * 2048 + oc * 128:c * 2048 + (oc + 1) * 128],
                           XBF[c][:, sl]) for c in range(NCH)]))
    for oc in range(NCH):
        for t in range(NT):
            sl = slice(t * TT, (t + 1) * TT)
            jobs.append((KPRE[oc][:, sl],
                         [(WQK[:, c * 2048 + 1024 + oc * 128:
                               c * 2048 + 1024 + (oc + 1) * 128],
                           XBF[c][:, sl]) for c in range(NCH)]))
    with nc.named_scope("qkprojd"):
        emit_proj(pr, jobs, waits=[(wld.h, wt_qk[-1])])
    s_qk.close()
    pr.tap("qpred", QPRE)
    if pr.stop_hit:
        return

    s_rt = contextlib.ExitStack()
    DC0 = R(s_rt, "DC0d", (128, T), F32)
    SGN = R(s_rt, "SGNd", (128, 1), F32)
    SEL = R(s_rt, "SELd", (16, 8 * 128), F32)
    LSC = R(s_rt, "LSCd", (16, 1), F32)
    SC16 = R(s_rt, "SC16d", (16, 1), F32)
    SRWD = R(s_rt, "SRWDd", (128, 1), F32)
    TMPF = R(s_rt, "TMPFd", (128, 2 * T), F32)
    COSB = R(s_rt, "COSBd", (128, T), BF16)
    SS2B = R(s_rt, "SS2Bd", (128, T), BF16)
    wt_rt = emit_loads(pr, [
        (DC0, dram["DCONST0"]), (SGN, dram["SGN"]),
        (SEL.rearrange("p (c n) -> p c n", n=128),
         dram["SELS"].rearrange("c p n -> p c n")),
        (LSC, dram["log_scale_c"]),
    ])
    with nc.Block(no_gpsimd_drain=True) as block:
        @block.scalar
        def _(scalar):
            scalar.wait_ge(wld.h, wt_rt[-1])
            scalar.activation(SC16, LSC, AF.Exp).then_inc(ac.h, 1)
    ac.inc(1)
    emit_barrier(nc)

    with contextlib.ExitStack() as ps_scope:
        ps_sr = ps_scope.enter_context(
            nc.psum_tensor(U("dsr"), [128, 1], F32))[:]
        for c in range(NCH):
            with nc.Block(no_gpsimd_drain=True) as block:
                @block.tensor
                def _(tensor):
                    tensor.matmul(ps_sr, SEL[:, c * 128:(c + 1) * 128], SC16,
                                  start=True, stop=True).then_inc(mm.h, 1)

                @block.vector
                def _(vector):
                    gg = pr.sems["gg"]
                    vector.wait_ge(mm.h, mm.inc())
                    vector.tensor_copy(SRWD, ps_sr).then_inc(gg.h, 1)
                    vector.wait_ge(gg.h, gg.inc())
                    vector.engine_nop().then_inc(ev.h, 1)
                    u = TMPF[:, 0:T]
                    r = TMPF[:, T:2 * T]
                    vector.tensor_scalar(u, DC0, SRWD, None, ALU.mult)
                    vector.tensor_scalar(r, u, float(2 ** 23), None, ALU.add)
                    vector.tensor_scalar(r, r, float(-(2 ** 23)), None, ALU.add)
                    vector.tensor_tensor(r, u, r, ALU.subtract)\
                        .then_inc(ev.h, 1)

                @block.scalar
                def _(scalar):
                    scalar.wait_ge(ev.h, ev.v + 2)
                    scalar.activation(SS2B, TMPF[:, T:2 * T], AF.Sin,
                                      scale=2 * math.pi).then_inc(ac.h, 1)

                @block.vector
                def _(vector):
                    vector.wait_ge(ac.h, ac.v + 1)
                    u = TMPF[:, 0:T]
                    r = TMPF[:, T:2 * T]
                    vector.tensor_scalar(u, u, 0.25, None, ALU.add)
                    vector.tensor_scalar(r, u, float(2 ** 23), None, ALU.add)
                    vector.tensor_scalar(r, r, float(-(2 ** 23)), None, ALU.add)
                    vector.tensor_tensor(r, u, r, ALU.subtract)\
                        .then_inc(ev.h, 1)

                @block.scalar
                def _(scalar):
                    scalar.wait_ge(ev.h, ev.v + 3)
                    scalar.activation(COSB, TMPF[:, T:2 * T], AF.Sin,
                                      scale=2 * math.pi).then_inc(ac.h, 1)

                @block.vector
                def _(vector):
                    vector.wait_ge(ac.h, ac.v + 2)
                    vector.tensor_scalar(SS2B, SS2B, SGN, None, ALU.mult)
            ev.inc(3)
            ac.inc(2)
            emit_rotate(pr, [QPRE[c], KPRE[c]], COSB, SS2B, b["SWAP64"],
                        f"d{c}")
    s_rt.close()
    pr.tap("qrotd", QPRE)
    pr.tap("krotd", KPRE)
    if pr.stop_hit:
        return

    VSB = L("VSBd", (128, 16 * 1024), BF16)
    s_v = contextlib.ExitStack()
    WVD = R(s_v, "WVDd", (128, 8 * 256), BF16)
    for q4 in range(4):
        wt_v = emit_loads(pr, [
            (WVD.rearrange("p (c n) -> p c n", n=256),
             r3(dram["w_qkv_p"][:, 2048 + q4 * 256:2048 + (q4 + 1) * 256]))])
        jobs = []
        for tc in range(16):
            for vh in range(2):
                vq = q4 * 256 + vh * 128
                jobs.append((VSB[:, tc * 1024 + vq:tc * 1024 + vq + 128],
                             [(XBF[c][:, tc * 128:(tc + 1) * 128],
                               WVD[:, c * 256 + vh * 128:
                                   c * 256 + (vh + 1) * 128])
                              for c in range(NCH)]))
        emit_proj(pr, jobs, waits=[(wld.h, wt_v[-1])])
    s_v.close()
    pr.tap("vsbd", [VSB[:, i * 1024:(i + 1) * 1024] for i in range(16)])
    if pr.stop_hit:
        return

    CPB = b["CPB"]
    emit_bounds(pr, QPRE, KPRE, b["BONES_D"],
                lambda ci: b["BONES_D"][:, ci * 16:(ci + 1) * 16], CPB)

    def dense_pieces(h, kc, qt):
        cg, rb = h // 2, 64 * (h % 2)
        ksl = slice(kc * 128, (kc + 1) * 128)
        qsl = slice(qt * TT, (qt + 1) * TT)
        return [(KPRE[cg][rb:rb + 64, ksl], QPRE[cg][rb:rb + 64, qsl],
                 (rb, 0))]

    with nc.named_scope("cored"):
        emit_attn_core(pr, dense_pieces, VSB, pr.dram_attscr, CPB)
    pr.tap("attyd",
           [pr.dram_attscr[c * 128:(c + 1) * 128, :] for c in range(NCH)])
    s_att.close()
    if pr.stop_hit:
        return

    with nc.named_scope("oprojd"):
        emit_outproj(pr, dram["w_o"], pr.dram_attscr, xscr, "rmw")
    pr.tap("accB", [xscr[c * 128:(c + 1) * 128, :] for c in range(NCH)])


def build_program(want_taps=(), stop_after=None):
    nc = bass.Bass("TRN2", target_bir_lowering=False)
    pr = Prog(nc, want_taps)
    pr.stop_after = stop_after
    dram = pr.dram

    def din(name, shape, dt=F32):
        dram[name] = nc.dram_tensor(name, list(shape), dt,
                                    kind="ExternalInput").ap()
        return dram[name]

    xT = din("xT", (D, T))
    din("kv_dT", (D, 256)); din("q_dT", (D, 256))
    din("ku_nP", (256, 1024)); din("qu_nP", (256, 1024))
    din("v_uT", (256, D)); din("rope_qT", (256, 512))
    din("rope_kT", (D, 512)); din("o_wT", (D, D))
    din("ln1w", (1, D)); din("ln2w", (1, D))
    din("w_qkv_p", (D, 3 * D)); din("w_o", (D, D))
    din("log_scale_c", (NH, 1))
    din("router_wT", (D, 7)); din("rbias", (7, 1))
    din("gT", (8, D, 2 * D)); din("uT", (8, D, 2 * D)); din("dT", (8, 2 * D, D))
    din("IDN", (128, 128), BF16); din("IDNF", (128, 128), F32)
    din("SWAP32", (128, 128), BF16); din("SWAP64", (128, 128), BF16)
    din("MASKS", (4, 128, TT), BF16)
    din("MASKS01", (4, 128, 2 * TT), BF16)
    din("BONES_M", (4, 128, 16), BF16); din("BONES_D", (8, 128, 16), BF16)
    din("PAB", (4, 128, 128), BF16)
    din("CS1", (128, T), BF16); din("CS2", (128, T), BF16)
    din("DCONST0", (128, T)); din("SGN", (128, 1)); din("SELS", (8, 16, 128))
    din("SEL7", (7, 7 * 128), BF16)
    outT = nc.dram_tensor("outT", [D, T], F32, kind="ExternalOutput").ap()
    xscr = nc.dram_tensor("xscr", [D, T], F32, kind="Internal").ap()
    pr.dram_attscr = nc.dram_tensor("attscr", [D, T], BF16, kind="Internal").ap()
    pr.dram_yscr = nc.dram_tensor("yscr", [D, T], BF16, kind="Internal").ap()

    res = contextlib.ExitStack()
    sb = lambda name, shape, dt: res.enter_context(
        nc.sbuf_tensor(name, list(shape), dt))

    for sname in ("ld", "wld", "ev", "mm", "ac", "st", "gg"):
        pr.sems[sname] = Sem(res.enter_context(nc.semaphore(sname)))
    b = pr.bufs
    b["ONE1"] = sb("ONE1", (128, 128), BF16)[:]
    b["ONEC"] = sb("ONEC", (128, 1), BF16)[:]
    for nme in ("IDN", "SWAP32", "SWAP64"):
        b[nme] = sb(nme + "_sb", (128, 128), BF16)[:]
    b["IDNF"] = sb("IDNF_sb", (128, 128), F32)[:]
    b["BONES_M"] = sb("BONES_M_sb", (128, 4 * 16), BF16)
    b["BONES_D"] = sb("BONES_D_sb", (128, 8 * 16), BF16)
    b["LN1W"] = sb("LN1W", (1, D), BF16)[:]
    b["LN2W"] = sb("LN2W", (1, D), BF16)[:]
    b["SQT"] = sb("SQT", (128, T), BF16)[:]
    b["ROWA"] = sb("ROWA", (1, T), F32)[:]
    b["ROWB"] = sb("ROWB", (1, T), F32)[:]
    b["RR"] = sb("RR", (1, T), BF16)[:]
    b["RN"] = sb("RN", (1, T), BF16)[:]
    b["SEL7"] = sb("SEL7_sb", (7, 7 * 128), BF16)[:]
    b["TMP1"] = sb("TMP1", (128, TT), F32)[:]
    b["CONSTS"] = sb("CONSTS", (128, 4), F32)[:]
    b["NRM"] = sb("NRM", (16, 8), F32)[:]
    b["C16B"] = sb("C16B", (16, 1), BF16)[:]
    b["ROW8F"] = sb("ROW8F", (1, 8), F32)[:]
    b["ROW8B"] = sb("ROW8B", (1, 8), BF16)[:]
    b["CPB"] = sb("CPB", (128, 8), F32)[:]
    b["PRTF"] = sb("PRTF", (128, 128), F32)[:]
    b["WGTB"] = sb("WGTB", (128, 128), BF16)[:]
    b["GSC"] = sb("GSC", (128, 8), F32)[:]
    b["GMSK"] = sb("GMSK", (128, 8), mybir.dt.int32)[:]
    b["BNEG"] = sb("BNEG", (128, 8), F32)[:]
    b["BZERO"] = sb("BZERO", (128, 8), F32)[:]
    b["GTMP"] = sb("GTMP", (128, 16), F32)[:]
    b["WRT"] = sb("WRT", (128, 8 * 7), BF16)[:]
    b["RB7"] = sb("RB7", (7, 1), F32)[:]

    xbf_t = [sb(f"XBF{c}", (128, T), BF16) for c in range(NCH)]
    XBF = [h[:] for h in xbf_t]
    pr.XBF = XBF

    with nc.Block(no_gpsimd_drain=True) as block:
        @block.gpsimd
        def _(gpsimd):
            ld = pr.sems["ld"]

            def load(dst, src):
                gpsimd.dma_start(out=dst, in_=src).then_inc(ld.h, 16)
                ld.inc(16)
            load(b["IDN"], dram["IDN"])
            load(b["IDNF"], dram["IDNF"])
            load(b["SWAP32"], dram["SWAP32"])
            load(b["SWAP64"], dram["SWAP64"])
            load(b["BONES_M"][:].rearrange("p (a h) -> p a h", h=16),
                 dram["BONES_M"].rearrange("a p h -> p a h"))
            load(b["BONES_D"][:].rearrange("p (a h) -> p a h", h=16),
                 dram["BONES_D"].rearrange("a p h -> p a h"))
            load(b["SEL7"], dram["SEL7"])
            load(b["LN1W"], dram["ln1w"])
            load(b["LN2W"], dram["ln2w"])
            gpsimd.wait_ge(ld.h, ld.v)

        @block.vector
        def _(vector):
            vector.memset(b["ONE1"], 1.0)
            vector.memset(b["ONEC"], 1.0)
            vector.memset(b["WGTB"], 0.0)
            vector.memset(b["BNEG"], -1e9)
            vector.memset(b["BZERO"], 0.0)
            vector.memset(b["CONSTS"][:, 0:1], EPS)
            vector.memset(b["CONSTS"][:, 1:2], math.log(0.125))
    emit_barrier(nc)

    def finish():
        with nc.Block(no_gpsimd_drain=True) as block:
            @block.sync
            def _(sync):
                sync.nop()
        pr._res = res
        return nc, pr

    emit_ffn(pr, "A", xT, xscr)
    if pr.stop_hit:
        return finish()

    emit_ln(pr, "hbm", xscr, XBF, None)
    pr.tap("yB", XBF)
    if pr.stop_hit:
        return finish()
    emit_dense(pr, xscr)
    if pr.stop_hit:
        return finish()

    emit_ffn(pr, "C", xscr, outT)
    pr._res = res
    return nc, pr


def emit_store_x(pr, acc, dst):
    nc = pr.nc
    st = pr.sems["st"]
    with nc.Block(no_gpsimd_drain=True) as block:
        @block.sync
        def _(sync):
            for c in range(NCH):
                sync.dma_start(out=dst[c * 128:(c + 1) * 128, :],
                               in_=acc[c]).then_inc(st.h, 16)
            st.inc(16 * NCH)
            sync.wait_ge(st.h, st.v)
    emit_barrier(nc)


_BUILD_CACHE = {}


def _get_program(want_taps=(), stop_after=None):
    key = (tuple(sorted(want_taps)), stop_after)
    if key not in _BUILD_CACHE:
        _BUILD_CACHE[key] = build_program(want_taps, stop_after)
    return _BUILD_CACHE[key]


def _run(inputs, want_taps=(), stop_after=None):
    consts = _host_consts()
    w = _prep_weights(inputs)
    x = np.asarray(inputs["L0_attn_x"], dtype=np.float32)
    nc, pr = _get_program(want_taps, stop_after)
    common = dict(w)
    for k, v in consts.items():
        common[k] = np.ascontiguousarray(v)
    in_maps = []
    for bb in range(x.shape[0]):
        m = dict(common)
        m["xT"] = np.ascontiguousarray(x[bb].T)
        in_maps.append(m)
    res = run_bass_kernel_spmd(nc, in_maps, core_ids=list(range(len(in_maps))))
    return res, pr


def kernel(**inputs):
    res, pr = _run(inputs)
    out = np.stack([np.ascontiguousarray(res.results[bb]["outT"].T)
                    for bb in range(len(res.results))], axis=0)
    return out
